# version 09
# baseline (speedup 1.0000x reference)
import numpy as np
import ml_dtypes
from contextlib import ExitStack
import concourse.bass as bass
import concourse.mybir as mybir
from concourse.bass_utils import run_bass_kernel_spmd

F32 = mybir.dt.float32
BF16 = mybir.dt.bfloat16
AF = mybir.ActivationFunctionType
ALU = mybir.AluOpType

S, D, H, HD = 4096, 1024, 16, 64
NT = S // 128
NST = S // 512
NP = H // 2
FF = 4096
MH = 2048

ENGS = ("t", "v", "s", "y", "g")
DBG = False

SEM_NAMES = [
    "bar", "pre", "tr", "trev",
    "xl0", "xl1", "xl2", "xr0", "xr1", "xr2", "xo0", "xo1", "xo2",
    "qt0", "qt1", "qt2", "sp0", "sp1", "gl0", "gl1", "gl2", "gl3", "gst0", "gst1",
    "wl0", "wl1", "wl2", "wd",
    "ln1", "ln2", "ln3", "xn", "pp", "mm", "evA", "xnew",
    "h1", "act1", "mm2", "b2", "bin1", "bin2",
    "qk", "exp", "pv", "rc", "rep", "bc", "ao", "wo", "tl1", "tl2", "kld",
    "ch1", "ch2", "g1", "g2", "g3", "vv", "sv",
]


class Sem:
    __slots__ = ("h", "n", "name")

    def __init__(self, h, name):
        self.h, self.n, self.name = h, 0, name


class RecInst:
    __slots__ = ("items", "idx")

    def __init__(self, items, idx):
        self.items, self.idx = items, idx

    def then_inc(self, sem, k):
        self.items[self.idx][3].append((sem, k))
        sem.n += k
        return self


class Rec:
    def __init__(self, items):
        self.__dict__["items"] = items

    def wait_ge(self, sem, v):
        assert v <= sem.n, f"wait {v} > expected {sem.n} on {sem.name}"
        if v > 0:
            self.items.append(("wait", sem, v, []))

    def __getattr__(self, name):
        items = self.items

        def call(*args, **kw):
            items.append(("call", name, (args, kw), []))
            return RecInst(items, len(items) - 1)

        return call


def replay(eng, items):
    for idx, (kind, a, b, incs) in enumerate(items):
        if kind == "wait":
            eng.wait_ge(a.h, b)
        else:
            try:
                inst = getattr(eng, a)(*b[0], **b[1])
            except Exception as e:
                raise RuntimeError(f"replay fail at item {idx}: {a} args={[getattr(x, 'shape', x) for x in b[0]]} kw={list(b[1])}") from e
            for sem, k in incs:
                inst = inst.then_inc(sem.h, k)


class Ctx:
    pass


C_EXPSHIFT, C_LNEPS, C_1PEPS, C_NEG1 = 0, 1, 2, 3



def ve(ctx, inst):
    inst.then_inc(ctx.S["vv"], 1)
    ctx.v.wait_ge(ctx.S["vv"], ctx.S["vv"].n)


def se(ctx, inst):
    inst.then_inc(ctx.S["sv"], 1)
    ctx.s.wait_ge(ctx.S["sv"], ctx.S["sv"].n)


def ln_tile(ctx, xap, xn_out):
    v, s, SS = ctx.v, ctx.s, ctx.S
    bn6, mv, sd, rstd = ctx.bn6, ctx.mv, ctx.sd, ctx.rstd
    v.bn_stats(bn6[:, 0:6], xap[:, 0:512])
    v.bn_stats(bn6[:, 6:12], xap[:, 512:1024]).then_inc(SS["ln1"], 1)
    v.wait_ge(SS["ln1"], SS["ln1"].n)
    v.bn_aggr(mv[:, 0:2], bn6[:, 0:12]).then_inc(SS["ln2"], 1)
    s.wait_ge(SS["ln2"], SS["ln2"].n)
    s.activation(sd[:, 0:1], mv[:, 1:2], AF.Sqrt,
                 bias=ctx.consts[:, C_LNEPS:C_LNEPS + 1]).then_inc(SS["ln3"], 1)
    v.wait_ge(SS["ln3"], SS["ln3"].n)
    ve(ctx, v.reciprocal(rstd[:, 0:1], sd[:, 0:1]))
    return v.tensor_scalar(xn_out, xap, mv[:, 0:1], rstd[:, 0:1], ALU.subtract, ALU.mult)



def ssq_via_bn(ctx, sp, out_col):
    v, SS = ctx.v, ctx.S
    v.bn_stats(ctx.bn6[:, 0:6], sp[:, 1:513])
    v.bn_stats(ctx.bn6[:, 6:12], sp[:, 513:1024]).then_inc(SS["ln1"], 1)
    v.wait_ge(SS["ln1"], SS["ln1"].n)
    ve(ctx, v.bn_aggr(ctx.mv[:, 0:2], ctx.bn6[:, 0:12]))
    ve(ctx, v.tensor_tensor(ctx.sd[:, 0:1], ctx.mv[:, 0:1], ctx.mv[:, 0:1], ALU.mult))
    ve(ctx, v.tensor_tensor(ctx.sd[:, 0:1], ctx.mv[:, 1:2], ctx.sd[:, 0:1], ALU.add))
    return v.tensor_scalar(out_col, ctx.sd[:, 0:1], 1023.0, None, ALU.mult)


def do_barrier(ctx):
    ctx.bar_round += 1
    tgt = 5 * ctx.bar_round
    for k in ENGS:
        getattr(ctx, k).nop().then_inc(ctx.S["bar"], 1)
    assert ctx.S["bar"].n == tgt
    for k in ENGS:
        getattr(ctx, k).wait_ge(ctx.S["bar"], tgt)


def transpose_rounds(ctx, srcs, dsts3d):
    t, v, SS = ctx.t, ctx.v, ctx.S
    n = len(srcs)
    for r in range((n + 3) // 4):
        bank = ctx.ps_tA if (ctx.tr_round % 2 == 0) else ctx.ps_tB
        if ctx.tr_round >= 2:
            t.wait_ge(SS["trev"], ctx.tr_round - 1)
        k = min(4, n - 4 * r)
        for j in range(k):
            inst = t.transpose(bank[:, j * 128:(j + 1) * 128], srcs[4 * r + j], ctx.ident)
        inst.then_inc(SS["tr"], 1)
        v.wait_ge(SS["tr"], SS["tr"].n)
        v.tensor_copy(dsts3d[r],
                      bank[:, 0:k * 128].rearrange("p (a b) -> p a b", b=128)
                      ).then_inc(SS["trev"], 1)
        ctx.tr_round += 1


def transpose_col(ctx, src_col_bf, dst_row):
    t, v, SS = ctx.t, ctx.v, ctx.S
    bank = ctx.ps_tA if (ctx.tr_round % 2 == 0) else ctx.ps_tB
    if ctx.tr_round >= 2:
        t.wait_ge(SS["trev"], ctx.tr_round - 1)
    t.transpose(bank[0:1, 0:128], src_col_bf, ctx.ident).then_inc(SS["tr"], 1)
    v.wait_ge(SS["tr"], SS["tr"].n)
    v.tensor_copy(dst_row, bank[0:1, 0:128]).then_inc(SS["trev"], 1)
    ctx.tr_round += 1


def wT_prepare(ctx, w_d, nch, cw, dstT, TW, perm=False):
    t, v, g, SS = ctx.t, ctx.v, ctx.g, ctx.S
    wst = ctx.wst
    swl = [SS["wl0"], SS["wl1"]]
    b_wd = SS["wd"].n
    G = max(1, 4096 // cw)
    nb = nch // G
    g.wait_ge(SS["trev"], SS["trev"].n)
    for b in range(nb):
        sl = b % 2
        if b >= 2:
            g.wait_ge(SS["wd"], b_wd + (b - 1) * G)
        if perm:
            src6 = w_d.rearrange("(bb c h r b2) i -> bb h b2 r c i",
                                 bb=nb, c=G, h=2, r=32, b2=2)
            for hh in range(2):
                for b2 in range(2):
                    base = hh * 64 + b2 * 32
                    g.dma_start(wst[base:base + 32, sl * 4096: sl * 4096 + G * cw]
                                .rearrange("p (c i) -> p c i", i=cw),
                                src6[b, hh, b2]).then_inc(swl[sl], 16)
        else:
            srcap = w_d.rearrange("(bb c p) i -> bb p c i", bb=nb, c=G, p=128)[b]
            g.dma_start(wst[:, sl * 4096: sl * 4096 + G * cw]
                        .rearrange("p (c i) -> p c i", i=cw), srcap).then_inc(swl[sl], 16)
        t.wait_ge(swl[sl], swl[sl].n)
        for ci in range(G):
            c = b * G + ci
            srcs = [wst[:, sl * 4096 + ci * cw + j * 128:
                        sl * 4096 + ci * cw + (j + 1) * 128]
                    for j in range(cw // 128)]
            ntr = cw // 128
            dsts = []
            for r in range((ntr + 3) // 4):
                k = min(4, ntr - 4 * r)
                dsts.append(dstT.rearrange("p (q c2) -> p q c2", c2=TW)
                            [:, 4 * r:4 * r + k, c * 128:(c + 1) * 128])
            transpose_rounds(ctx, srcs, dsts)
            v.nop().then_inc(SS["wd"], 1)



def load_bias_cols(ctx, src_1d, nch, dst_col, stg, ps_f32):
    t, v, y, SS = ctx.t, ctx.v, ctx.y, ctx.S
    y.dma_start(stg[0:nch, :], src_1d.rearrange("(c p) -> c p", p=128)).then_inc(SS["b2"], 16)
    t.wait_ge(SS["b2"], SS["b2"].n)
    t.transpose(ps_f32[:, 0:nch], stg[0:nch, :], ctx.identf[0:nch, 0:nch]
                ).then_inc(SS["b2"], 1)
    v.wait_ge(SS["b2"], SS["b2"].n)
    v.tensor_copy(dst_col, ps_f32[:, 0:nch]).then_inc(SS["b2"], 1)
    t.wait_ge(SS["b2"], SS["b2"].n)


def build(nph=4, phase_list=("a", "f", "b", "f")):
    nc = bass.Bass("TRN2", target_bir_lowering=False)
    ctx = Ctx()
    ctx.nc = nc
    ctx.work = {k: [] for k in ENGS}
    for k in ENGS:
        setattr(ctx, k, Rec(ctx.work[k]))
    ctx.stack = ExitStack()
    ctx.bar_round = 0
    ctx.tr_round = 0
    ctx.S = {}
    for nm in SEM_NAMES:
        h = ctx.stack.enter_context(nc.semaphore(nm))
        ctx.S[nm] = Sem(h, nm)

    dri = lambda n, sh, dt=F32: nc.dram_tensor(n, sh, dt, kind="ExternalInput").ap()
    x_in = dri("x", [S, D])
    w = {k: dri(k, [D, D]) for k in ("wq", "wk", "wv", "wo", "bWq", "bWk", "bWv", "bWo")}
    w["fw1"] = dri("fw1", [FF, D]); w["fb1"] = dri("fb1", [FF])
    w["fw2"] = dri("fw2", [D, FF]); w["fb2"] = dri("fb2", [D])
    w["mW1"] = dri("mW1", [MH, D]); w["mb1"] = dri("mb1", [MH])
    w["mW2"] = dri("mW2", [D, MH]); w["mb2"] = dri("mb2", [D])
    w["fb1c"] = dri("fb1c", [128, 32]); w["mb1c"] = dri("mb1c", [128, 16])
    ident_d = dri("ident", [128, 128], BF16)
    identf_d = dri("identf", [128, 128])
    ones_bf_d = dri("ones_bf", [1, 128], BF16)
    ones_f_d = dri("ones_f", [1, 128])
    consts_d = dri("consts", [128, 8])
    ctok_d = dri("ctok", [S, 32])
    stok_d = dri("stok", [S, 32])
    out_d = nc.dram_tensor("out", [S, D], F32, kind="ExternalOutput").ap()
    xA = nc.dram_tensor("xA", [S, D], F32, kind="Internal").ap()
    xB = nc.dram_tensor("xB", [S, D], F32, kind="Internal").ap()
    spkind = "ExternalOutput" if DBG else "Internal"
    qsp = nc.dram_tensor("qsp", [NP, 128, S], BF16, kind=spkind).ap()
    ksp = nc.dram_tensor("ksp", [NP, 128, S], BF16, kind=spkind).ap()
    gsp = nc.dram_tensor("gsp", [NT, 128, D], BF16, kind=spkind).ap()
    bw1T_d = nc.dram_tensor("bw1T_d", [128, 8 * FF], BF16, kind="Internal").ap()
    bw2T_d = nc.dram_tensor("bw2T_d", [128, 32 * D], BF16, kind="Internal").ap()

    sb = lambda n, sh, dt: ctx.stack.enter_context(nc.sbuf_tensor(n, sh, dt))
    ctx.ident = sb("identb", [128, 128], BF16)[:]
    ctx.identf = sb("identf_sb", [128, 128], F32)[:]
    ctx.ones_bf = sb("onesb", [1, 128], BF16)[:]
    ctx.ones_f = sb("onesf", [1, 128], F32)[:]
    ctx.consts = sb("consts_sb", [128, 8], F32)[:]
    ctx.bn6 = sb("bn6", [128, 12], F32)[:]
    ctx.mv = sb("mv", [128, 2], F32)[:]
    ctx.sd = sb("sd", [128, 1], F32)[:]
    ctx.rstd = sb("rstd", [128, 1], F32)[:]
    ctx.wst = sb("wst", [128, 2 * 4096], BF16)[:]

    SS = ctx.S
    y = ctx.y
    y.dma_start(ctx.ident, ident_d).then_inc(SS["pre"], 16)
    y.dma_start(ctx.identf, identf_d).then_inc(SS["pre"], 16)
    y.dma_start(ctx.ones_bf, ones_bf_d).then_inc(SS["pre"], 16)
    y.dma_start(ctx.ones_f, ones_f_d).then_inc(SS["pre"], 16)
    y.dma_start(ctx.consts, consts_d).then_inc(SS["pre"], 16)
    ctx.ctok_d, ctx.stok_d = ctok_d, stok_d
    for k in ENGS:
        getattr(ctx, k).wait_ge(SS["pre"], 80)

    bufs = [x_in]
    for i in range(nph - 1):
        bufs.append(xA if i % 2 == 0 else xB)
    bufs.append(out_d)

    prepped = False
    for i, ph in enumerate(phase_list[:nph]):
        src, dst = bufs[i], bufs[i + 1]
        if ph in ("a", "b"):
            ph_attn(ctx, src, dst, w, qsp, ksp, gsp, kind=ph)
        else:
            ph_ffn(ctx, src, dst, w, bw1T_d, bw2T_d, prep=not prepped)
            prepped = True
        do_barrier(ctx)

    with ctx.stack:
        with nc.Block() as block:
            @block.tensor
            def _(e):
                replay(e, ctx.work["t"])

            @block.vector
            def _(e):
                replay(e, ctx.work["v"])

            @block.scalar
            def _(e):
                replay(e, ctx.work["s"])

            @block.sync
            def _(e):
                replay(e, ctx.work["y"])

            @block.gpsimd
            def _(e):
                replay(e, ctx.work["g"])
    return nc


def ph_ffn(ctx, src, dst, w, bw1T_d, bw2T_d, prep):
    nc = ctx.nc
    t, v, s, y, g, SS = ctx.t, ctx.v, ctx.s, ctx.y, ctx.g, ctx.S
    STW = 256
    NSTF = S // STW
    with ExitStack() as ph:
        psb = lambda n, sh, dt=F32: ph.enter_context(nc.psum_tensor(n, sh, dt))[:]
        ssb = lambda n, sh, dt: ph.enter_context(nc.sbuf_tensor(n, sh, dt))[:]
        uid = f"F{ctx.bar_round}_"

        bw1T = ssb(uid + "bw1T", [128, 8 * FF], BF16)
        bw2T = ssb(uid + "bw2T", [128, 32 * D], BF16)
        b1col = ssb(uid + "b1col", [128, 32], F32)
        b2bc = ssb(uid + "b2bc", [128, D], F32)

        ps_h = [psb(uid + "psh0", [128, 512]), psb(uid + "psh1", [128, 512])]
        ps_o = [psb(uid + "pso0", [128, 1024]), psb(uid + "pso1", [128, 1024])]
        ctx.ps_tA = psb(uid + "ptA", [128, 1024], BF16)
        ctx.ps_tB = psb(uid + "ptB", [128, 1024], BF16)

        sxl = [SS["xl0"], SS["xl1"], SS["xl2"]]
        sxr = [SS["xr0"], SS["xr1"], SS["xr2"]]
        sxo = [SS["xo0"], SS["xo1"], SS["xo2"]]

        if prep:
            with ExitStack() as pp:
                bwc = pp.enter_context(nc.sbuf_tensor(uid + "bwc", [128, 2 * 4096], BF16))[:]
                alpha = pp.enter_context(nc.sbuf_tensor(uid + "alpha", [128, 1], F32))[:]
                swl = [SS["wl0"], SS["wl1"]]
                b_bin1, b_bin2 = SS["bin1"].n, SS["bin2"].n
                chunk_trev = []
                slot_i = 0
                g.wait_ge(SS["trev"], SS["trev"].n)
                for name, nch, cw, dstT, TW in (("fw1", 32, 1024, bw1T, FF),
                                                ("fw2", 8, 4096, bw2T, D)):
                    for c in range(nch):
                        sl = slot_i % 2
                        if slot_i >= 2:
                            g.wait_ge(SS["bin1"], b_bin1 + slot_i - 1)
                        g.dma_start(ctx.wst[:, sl * 4096: sl * 4096 + cw],
                                    w[name].rearrange("(c p) i -> c p i", p=128)[c]
                                    ).then_inc(swl[sl], 16)
                        s.wait_ge(swl[sl], swl[sl].n)
                        bsl = slot_i % 2
                        if slot_i >= 2:
                            s.wait_ge(SS["trev"], chunk_trev[slot_i - 2])
                        s.activation(bwc[:, bsl * 4096: bsl * 4096 + cw],
                                     ctx.wst[:, sl * 4096: sl * 4096 + cw],
                                     AF.Abs, accum_out=alpha[:, 0:1])
                        s.activation(bwc[:, bsl * 4096: bsl * 4096 + cw],
                                     ctx.wst[:, sl * 4096: sl * 4096 + cw],
                                     AF.Sign).then_inc(SS["bin1"], 1)
                        v.wait_ge(SS["bin1"], SS["bin1"].n)
                        v.tensor_scalar(bwc[:, bsl * 4096: bsl * 4096 + cw],
                                        bwc[:, bsl * 4096: bsl * 4096 + cw],
                                        alpha[:, 0:1], 1.0 / cw,
                                        ALU.mult, ALU.mult).then_inc(SS["bin2"], 1)
                        t.wait_ge(SS["bin2"], SS["bin2"].n)
                        srcs = [bwc[:, bsl * 4096 + j * 128: bsl * 4096 + (j + 1) * 128]
                                for j in range(cw // 128)]
                        ntr = cw // 128
                        dsts = []
                        for r in range((ntr + 3) // 4):
                            kk = min(4, ntr - 4 * r)
                            dsts.append(dstT.rearrange("p (q c2) -> p q c2", c2=TW)
                                        [:, 4 * r:4 * r + kk, c * 128:(c + 1) * 128])
                        transpose_rounds(ctx, srcs, dsts)
                        chunk_trev.append(SS["trev"].n)
                        slot_i += 1
                y.wait_ge(SS["trev"], SS["trev"].n)
                y.dma_start(bw1T_d, bw1T).then_inc(SS["b2"], 16)
                y.dma_start(bw2T_d, bw2T).then_inc(SS["b2"], 16)
        else:
            y.dma_start(bw1T, bw1T_d).then_inc(SS["b2"], 16)
            y.dma_start(bw2T, bw2T_d).then_inc(SS["b2"], 16)
        y.dma_start(b1col, w["fb1c"]).then_inc(SS["b2"], 16)
        y.dma_start(b2bc[0:1, :], w["fb2"].rearrange("(o d) -> o d", o=1)).then_inc(SS["b2"], 16)
        t.wait_ge(SS["b2"], SS["b2"].n)
        t.matmul(ps_o[0][:, 0:512], ctx.ones_f, b2bc[0:1, 0:512], start=True, stop=True)
        t.matmul(ps_o[0][:, 512:1024], ctx.ones_f, b2bc[0:1, 512:1024],
                 start=True, stop=True).then_inc(SS["b2"], 1)
        v.wait_ge(SS["b2"], SS["b2"].n)
        v.tensor_copy(b2bc, ps_o[0]).then_inc(SS["b2"], 1)
        t.wait_ge(SS["b2"], SS["b2"].n)

        h1T = ssb(uid + "h1T", [128, 32 * STW], BF16)
        xnT = ssb(uid + "xnT", [128, 8 * STW], BF16)
        xn_b = ssb(uid + "xn", [128, 4 * 1024], BF16)
        xl = ssb(uid + "xl", [128, 2 * 1024], F32)
        xr = ssb(uid + "xr", [128, 2 * 1024], F32)
        xo = ssb(uid + "xo", [128, 2 * 1024], F32)

        xs = src.rearrange("(n p) d -> n p d", p=128)
        ds = dst.rearrange("(n p) d -> n p d", p=128)
        b_xn = SS["xn"].n
        b_xnew = SS["xnew"].n
        b_gelu = SS["act1"].n
        xn_tr_mark = []
        sxo_mark = [0, 0, 0]
        y.wait_ge(SS["trev"], SS["trev"].n)

        for st in range(NSTF):
            for tt in range(2):
                ti = st * 2 + tt
                sl = ti % 2
                if ti >= 2:
                    y.wait_ge(SS["xn"], b_xn + ti - 1)
                y.dma_start(xl[:, sl * 1024:(sl + 1) * 1024], xs[ti]).then_inc(sxl[sl], 16)
                v.wait_ge(sxl[sl], sxl[sl].n)
                if ti >= 4:
                    v.wait_ge(SS["tr"], xn_tr_mark[ti - 4])
                v.wait_ge(SS["h1"], SS["h1"].n)
                xb = xn_b[:, (ti % 4) * 1024:(ti % 4 + 1) * 1024]
                ln_tile(ctx, xl[:, sl * 1024:(sl + 1) * 1024], xb).then_inc(SS["xn"], 1)
                t.wait_ge(SS["xn"], SS["xn"].n)
                srcs = [xb[:, j * 128:(j + 1) * 128] for j in range(8)]
                dsts = []
                for r in range(2):
                    dsts.append(xnT.rearrange("p (ic q) -> p ic q", q=STW)
                                [:, 4 * r:4 * r + 4, tt * 128:(tt + 1) * 128])
                transpose_rounds(ctx, srcs, dsts)
                xn_tr_mark.append(SS["tr"].n)
            t.wait_ge(SS["trev"], SS["trev"].n)
            for hc in range(32):
                bank = hc % 2
                if SS["act1"].n - b_gelu >= 2:
                    t.wait_ge(SS["act1"], SS["act1"].n - 1)
                for ic in range(8):
                    inst = t.matmul(ps_h[bank][:, 0:STW],
                                    bw1T[:, ic * FF + hc * 128: ic * FF + (hc + 1) * 128],
                                    xnT[:, ic * STW:(ic + 1) * STW],
                                    start=(ic == 0), stop=(ic == 7))
                inst.then_inc(SS["h1"], 1)
                s.wait_ge(SS["h1"], SS["h1"].n)
                s.activation(h1T[:, hc * STW:(hc + 1) * STW], ps_h[bank][:, 0:STW], AF.Gelu,
                             bias=b1col[:, hc:hc + 1]).then_inc(SS["act1"], 1)
            t.wait_ge(SS["act1"], SS["act1"].n)
            for tt in range(2):
                ti = st * 2 + tt
                bank = tt % 2
                if ti >= 2:
                    t.wait_ge(SS["xnew"], b_xnew + ti - 1)
                for hc in range(32):
                    t.matmul(ps_o[bank][:, 0:512],
                             h1T[:, hc * STW + tt * 128: hc * STW + (tt + 1) * 128],
                             bw2T[:, hc * D: hc * D + 512],
                             start=(hc == 0), stop=(hc == 31))
                    inst = t.matmul(ps_o[bank][:, 512:1024],
                                    h1T[:, hc * STW + tt * 128: hc * STW + (tt + 1) * 128],
                                    bw2T[:, hc * D + 512: hc * D + 1024],
                                    start=(hc == 0), stop=(hc == 31))
                inst.then_inc(SS["mm2"], 1)
                slr = ti % 2
                if ti >= 2:
                    y.wait_ge(SS["xnew"], b_xnew + ti - 1)
                y.dma_start(xr[:, slr * 1024:(slr + 1) * 1024], xs[ti]).then_inc(sxr[slr], 16)
                v.wait_ge(SS["mm2"], SS["mm2"].n)
                v.wait_ge(sxr[slr], sxr[slr].n)
                slo = ti % 2
                if ti >= 2:
                    v.wait_ge(sxo[slo], sxo_mark[slo])
                xov = xo[:, slo * 1024:(slo + 1) * 1024]
                v.tensor_tensor(xov, ps_o[bank], b2bc, ALU.add)
                v.tensor_tensor(xov, xov, xr[:, slr * 1024:(slr + 1) * 1024], ALU.add
                                ).then_inc(SS["xnew"], 1)
                y.wait_ge(SS["xnew"], SS["xnew"].n)
                y.dma_start(ds[ti], xov).then_inc(sxo[slo], 16)
                sxo_mark[slo] = sxo[slo].n
        for sl in range(2):
            y.wait_ge(sxo[sl], sxo[sl].n)
        if DBG:
            dmp = {"d_bw1T": bw1T, "d_bw2T": bw2T, "d_b1col": b1col,
                   "d_b2bc": b2bc, "d_h1T": h1T, "d_xnT": xnT}
            for nm, ap in dmp.items():
                dd = nc.dram_tensor(nm, list(ap.shape),
                                    ap.dtype, kind="ExternalOutput").ap()
                y.wait_ge(SS["xnew"], SS["xnew"].n)
                y.dma_start(dd, ap).then_inc(SS["b2"], 16)
            y.wait_ge(SS["b2"], SS["b2"].n)


def ph_attn(ctx, src, dst, w, qsp, ksp, gsp, kind):
    nc = ctx.nc
    t, v, s, y, g, SS = ctx.t, ctx.v, ctx.s, ctx.y, ctx.g, ctx.S
    xs = src.rearrange("(n p) d -> n p d", p=128)
    ds = dst.rearrange("(n p) d -> n p d", p=128)
    uid = f"A{ctx.bar_round}{kind}_"
    sxl = [SS["xl0"], SS["xl1"], SS["xl2"]]
    sxo = [SS["xo0"], SS["xo1"], SS["xo2"]]

    if kind == "b":
        with ExitStack() as smk:
            psbm = lambda n, sh, dt=F32: smk.enter_context(nc.psum_tensor(n, sh, dt))[:]
            ssbm = lambda n, sh, dt: smk.enter_context(nc.sbuf_tensor(n, sh, dt))[:]
            w1T = ssbm(uid + "w1T", [128, 8 * MH], BF16)
            w2T = ssbm(uid + "w2T", [128, 16 * D], BF16)
            b1c = ssbm(uid + "b1c", [128, 16], F32)
            b2bc = ssbm(uid + "b2bc", [128, D], F32)
            xnTm = ssbm(uid + "xnTm", [128, 2 * 8 * 512], BF16)
            xn_b = ssbm(uid + "xnb", [128, 4 * 1024], BF16)
            h1Tm = ssbm(uid + "h1Tm", [128, 16 * 512], BF16)
            tfp = ssbm(uid + "tfp", [128, 1024], F32)
            th = ssbm(uid + "th", [128, 1024], BF16)
            u = ssbm(uid + "u", [128, 1024], F32)
            u0 = ssbm(uid + "u0", [128, 1], F32)
            a0 = ssbm(uid + "a0", [128, 1], F32)
            gt = ssbm(uid + "gt", [128, 2 * 1024], BF16)
            xl = ssbm(uid + "xlm", [128, 3 * 1024], F32)
            ps_h = [psbm(uid + "ph0", [128, 512]), psbm(uid + "ph1", [128, 512])]
            ps_o = [psbm(uid + "po0", [128, 1024]), psbm(uid + "po1", [128, 1024])]
            ctx.ps_tA = psbm(uid + "ptAm", [128, 1024], BF16)
            ctx.ps_tB = psbm(uid + "ptBm", [128, 1024], BF16)
            sgst = [SS["gst0"], SS["gst1"]]

            wT_prepare(ctx, w["mW1"], 16, 1024, w1T, MH)
            wT_prepare(ctx, w["mW2"], 8, 2048, w2T, D)
            y.dma_start(b1c, w["mb1c"]).then_inc(SS["b2"], 16)
            y.dma_start(b2bc[0:1, :], w["mb2"].rearrange("(o d) -> o d", o=1)).then_inc(SS["b2"], 16)
            t.wait_ge(SS["b2"], SS["b2"].n)
            t.wait_ge(SS["trev"], SS["trev"].n)
            t.matmul(ps_o[0][:, 0:512], ctx.ones_f, b2bc[0:1, 0:512],
                     start=True, stop=True)
            t.matmul(ps_o[0][:, 512:1024], ctx.ones_f, b2bc[0:1, 512:1024],
                     start=True, stop=True).then_inc(SS["b2"], 1)
            v.wait_ge(SS["b2"], SS["b2"].n)
            v.tensor_copy(b2bc, ps_o[0]).then_inc(SS["b2"], 1)
            t.wait_ge(SS["b2"], SS["b2"].n)

            b_xn = SS["xn"].n
            b_th = SS["act1"].n
            sgst_mark = [0, 0]
            xn_tr_mark = []
            for st in range(NST):
                par = st % 2
                for tt in range(4):
                    ti = st * 4 + tt
                    sl = ti % 3
                    if ti >= 3:
                        y.wait_ge(SS["xn"], b_xn + ti - 2)
                    y.dma_start(xl[:, sl * 1024:(sl + 1) * 1024], xs[ti]
                                ).then_inc(sxl[sl], 16)
                    v.wait_ge(sxl[sl], sxl[sl].n)
                    if ti >= 4:
                        v.wait_ge(SS["tr"], xn_tr_mark[ti - 4])
                    xb = xn_b[:, (ti % 4) * 1024:(ti % 4 + 1) * 1024]
                    ln_tile(ctx, xl[:, sl * 1024:(sl + 1) * 1024], xb).then_inc(SS["xn"], 1)
                    t.wait_ge(SS["xn"], SS["xn"].n)
                    srcs = [xb[:, j * 128:(j + 1) * 128] for j in range(8)]
                    dsts = []
                    for r in range(2):
                        dsts.append(xnTm.rearrange("p (b2 ic q) -> p (b2 ic) q",
                                                   b2=2, q=512)
                                    [:, par * 8 + 4 * r: par * 8 + 4 * r + 4,
                                     tt * 128:(tt + 1) * 128])
                    transpose_rounds(ctx, srcs, dsts)
                    xn_tr_mark.append(SS["tr"].n)
                    v.wait_ge(SS["trev"], SS["trev"].n)
                    r0 = xnTm[0:1, par * 4096 + tt * 128: par * 4096 + (tt + 1) * 128]
                    v.tensor_scalar(r0, r0, -1.0, None, ALU.add).then_inc(SS["pp"], 1)
                t.wait_ge(SS["pp"], SS["pp"].n)
                for hc in range(16):
                    bank = hc % 2
                    if SS["act1"].n - b_th >= 2:
                        t.wait_ge(SS["act1"], SS["act1"].n - 1)
                    for ic in range(8):
                        inst = t.matmul(ps_h[bank],
                                        w1T[:, ic * MH + hc * 128: ic * MH + (hc + 1) * 128],
                                        xnTm[:, par * 4096 + ic * 512:
                                             par * 4096 + (ic + 1) * 512],
                                        start=(ic == 0), stop=(ic == 7))
                    inst.then_inc(SS["h1"], 1)
                    s.wait_ge(SS["h1"], SS["h1"].n)
                    s.activation(h1Tm[:, hc * 512:(hc + 1) * 512], ps_h[bank], AF.Tanh,
                                 bias=b1c[:, hc:hc + 1]).then_inc(SS["act1"], 1)
                t.wait_ge(SS["act1"], SS["act1"].n)
                for tt in range(4):
                    ti = st * 4 + tt
                    bank = tt % 2
                    if ti >= 2:
                        t.wait_ge(SS["mm2"], SS["mm2"].n - 1)
                    for hc in range(16):
                        t.matmul(ps_o[bank][:, 0:512],
                                 h1Tm[:, hc * 512 + tt * 128: hc * 512 + (tt + 1) * 128],
                                 w2T[:, hc * D: hc * D + 512],
                                 start=(hc == 0), stop=(hc == 15))
                        inst = t.matmul(ps_o[bank][:, 512:1024],
                                        h1Tm[:, hc * 512 + tt * 128:
                                             hc * 512 + (tt + 1) * 128],
                                        w2T[:, hc * D + 512: hc * D + 1024],
                                        start=(hc == 0), stop=(hc == 15))
                    inst.then_inc(SS["mm2"], 1)
                    v.wait_ge(SS["mm2"], SS["mm2"].n)
                    v.tensor_tensor(tfp, ps_o[bank], b2bc, ALU.add).then_inc(SS["mm2"], 1)
                    s.wait_ge(SS["mm2"], SS["mm2"].n)
                    s.activation(th, tfp, AF.Tanh).then_inc(SS["act1"], 1)
                    gsl = ti % 2
                    v.wait_ge(SS["act1"], SS["act1"].n)
                    if ti >= 2:
                        v.wait_ge(sgst[gsl], sgst_mark[gsl])
                    v.tensor_scalar(u, th, 0.1, 1.0, ALU.mult, ALU.add)
                    v.tensor_scalar(u0, th[:, 0:1], 0.1, -1.0, ALU.mult, ALU.add
                                    ).then_inc(SS["g1"], 1)
                    s.wait_ge(SS["g1"], SS["g1"].n)
                    if ti >= 2:
                        s.wait_ge(sgst[gsl], sgst_mark[gsl])
                    s.activation(gt[:, gsl * 1024:(gsl + 1) * 1024], u, AF.Abs)
                    s.activation(a0, u0, AF.Abs).then_inc(SS["g2"], 1)
                    v.wait_ge(SS["g2"], SS["g2"].n)
                    v.tensor_scalar(gt[:, gsl * 1024: gsl * 1024 + 1], a0, -1.0, None,
                                    ALU.mult).then_inc(SS["g3"], 1)
                    y.wait_ge(SS["g3"], SS["g3"].n)
                    y.dma_start(gsp[ti], gt[:, gsl * 1024:(gsl + 1) * 1024]
                                ).then_inc(sgst[gsl], 16)
                    sgst_mark[gsl] = sgst[gsl].n
            for i_ in range(2):
                y.wait_ge(sgst[i_], sgst[i_].n)
        do_barrier(ctx)

    with ExitStack() as paf:
        ssaf = lambda n, sh, dt: paf.enter_context(nc.sbuf_tensor(n, sh, dt))[:]
        vaug = ssaf(uid + "vaug", [128, NT * 1040], BF16)
        woT = ssaf(uid + "woT", [128, 8 * 1024], BF16)

        with ExitStack() as sak:
            psba = lambda n, sh, dt=F32: sak.enter_context(nc.psum_tensor(n, sh, dt))[:]
            ssba = lambda n, sh, dt: sak.enter_context(nc.sbuf_tensor(n, sh, dt))[:]
            if kind == "a":
                wq_d, wk_d, wv_d, wo_d = w["wq"], w["wk"], w["wv"], w["wo"]
            else:
                wq_d, wk_d, wv_d, wo_d = w["bWq"], w["bWk"], w["bWv"], w["bWo"]
            wqT = ssba(uid + "wqT", [128, 8 * 1024], BF16)
            wkT = ssba(uid + "wkT", [128, 8 * 1024], BF16)
            wvT = ssba(uid + "wvT", [128, 8 * 1024], BF16)
            xnT_t = ssba(uid + "xnTt", [128, 4 * 1024], BF16)
            qksb = ssba(uid + "qksb", [128, 1024], BF16)
            qrot = ssba(uid + "qrot", [128, 1024], BF16)
            krot = ssba(uid + "krot", [128, 1024], BF16)
            qstage = ssba(uid + "qstg", [128, 1024], BF16)
            kstage = ssba(uid + "kstg", [128, 1024], BF16)
            xl = ssba(uid + "xla", [128, 2 * 1024], F32)
            if kind == "a":
                xn_b = ssba(uid + "xnb2", [128, 4 * 1024], BF16)
                rt1 = ssba(uid + "rt1", [128, 512], BF16)
                rt2 = ssba(uid + "rt2", [128, 512], BF16)
                ctok = ssba(uid + "ctok", [128, NT * 32], F32)
                stok = ssba(uid + "stok", [128, NT * 32], F32)
                y.dma_start(ctok.rearrange("p (n r) -> p n r", r=32),
                            ctx.ctok_d.rearrange("(n p) r -> p n r", p=128)
                            ).then_inc(SS["b2"], 16)
                y.dma_start(stok.rearrange("p (n r) -> p n r", r=32),
                            ctx.stok_d.rearrange("(n p) r -> p n r", p=128)
                            ).then_inc(SS["b2"], 16)
                v.wait_ge(SS["b2"], SS["b2"].n)
            else:
                xn_f = ssba(uid + "xnf", [128, 1024], F32)
                spbf = ssba(uid + "spbf", [128, 4 * 1024], BF16)
                ssqc = ssba(uid + "ssqc", [128, 4], F32)
                y0c = ssba(uid + "y0c", [128, 4], F32)
                mc = ssba(uid + "mc", [128, 4], F32)
                t4c = ssba(uid + "t4c", [128, 4], F32)
                dnc = ssba(uid + "dnc", [128, 4], F32)
                dcc = ssba(uid + "dcc", [128, 4], F32)
                acol = ssba(uid + "acol", [128, NT], F32)
                d0col = ssba(uid + "d0col", [128, NT], F32)
                d0bf = ssba(uid + "d0bf", [128, 1], BF16)
                gtl = ssba(uid + "gtl", [128, 4 * 1024], BF16)
            ps_q = psba(uid + "psq", [128, 1024])
            ps_k = psba(uid + "psk", [128, 1024])
            ps_v = psba(uid + "psv", [128, 1024])
            ctx.ps_tA = psba(uid + "ptAa", [128, 1024], BF16)
            ctx.ps_tB = psba(uid + "ptBa", [128, 1024], BF16)
            ssp = [SS["sp0"], SS["sp1"]]
            sgl = [SS["gl0"], SS["gl1"], SS["gl2"], SS["gl3"]]

            wT_prepare(ctx, wq_d, 8, 1024, wqT, 1024, perm=(kind == "a"))
            wT_prepare(ctx, wk_d, 8, 1024, wkT, 1024, perm=(kind == "a"))
            wT_prepare(ctx, wv_d, 8, 1024, wvT, 1024)
            wT_prepare(ctx, wo_d, 8, 1024, woT, 1024)

            v.memset(vaug.rearrange("p (c h e) -> p (c h) e", c=NT, h=16)[:, :, 64:65], 1.0)
            v.nop().then_inc(SS["evA"], 1)

            b_xn = SS["xn"].n
            b_evA = SS["evA"].n
            tr_mark = [0] * NT
            xn_mark = [0] * NT
            pend_tr = [None]
            ssp_mark = [0, 0]
            sgl_mark = [0] * NT
            def load_g(ti):
                gl = ti % 4
                if ti >= 4:
                    y.wait_ge(SS["evA"], b_evA + 3 * ti - 11)
                y.dma_start(gtl[:, gl * 1024:(gl + 1) * 1024], gsp[ti]
                            ).then_inc(sgl[gl], 16)
                sgl_mark[ti] = sgl[gl].n
            def part_a(ti):
                tt = ti % 4
                st = ti // 4
                sl = ti % 2
                if ti >= 2:
                    y.wait_ge(SS["xn"], b_xn + ti - 1)
                y.dma_start(xl[:, sl * 1024:(sl + 1) * 1024], xs[ti]).then_inc(sxl[sl], 16)
                pass
                v.wait_ge(sxl[sl], sxl[sl].n)
                if ti >= 4:
                    v.wait_ge(SS["tr"], tr_mark[ti - 4])
                if kind == "a":
                    xb = xn_b[:, (ti % 4) * 1024:(ti % 4 + 1) * 1024]
                    ln_tile(ctx, xl[:, sl * 1024:(sl + 1) * 1024], xb).then_inc(SS["xn"], 1)
                    xn_mark[ti] = SS["xn"].n
                    src_t = xb
                else:
                    ln_tile(ctx, xl[:, sl * 1024:(sl + 1) * 1024], xn_f)
                    spb = spbf[:, (ti % 4) * 1024:(ti % 4 + 1) * 1024]
                    v.tensor_scalar(spb, xn_f, 8.0, -8.0, ALU.min, ALU.max
                                    ).then_inc(SS["xn"], 1)
                    xn_mark[ti] = SS["xn"].n
                    ssq_via_bn(ctx, spb, ssqc[:, tt:tt + 1])
                    src_t = spb
                pend_tr[0] = (src_t, ti)
                if kind == "b" and tt == 3:
                    v.nop().then_inc(SS["ch1"], 1)
                    s.wait_ge(SS["ch1"], SS["ch1"].n)
                    s.activation(y0c, ssqc, AF.Sqrt,
                                 bias=ctx.consts[:, C_1PEPS:C_1PEPS + 1]).then_inc(SS["ch2"], 1)
                    v.wait_ge(SS["ch2"], SS["ch2"].n)
                    ve(ctx, v.tensor_scalar(mc, y0c, 1.0 + 1e-6, None, ALU.max))
                    v.tensor_mul(t4c, mc, mc).then_inc(SS["ch1"], 1)
                    s.wait_ge(SS["ch1"], SS["ch1"].n)
                    s.activation(t4c, t4c, AF.Sqrt,
                                 bias=ctx.consts[:, C_NEG1:C_NEG1 + 1]).then_inc(SS["ch2"], 1)
                    v.wait_ge(SS["ch2"], SS["ch2"].n)
                    ve(ctx, v.tensor_sub(d0col[:, st * 4:st * 4 + 4], y0c, mc))
                    ve(ctx, v.tensor_mul(dnc, d0col[:, st * 4:st * 4 + 4],
                                         d0col[:, st * 4:st * 4 + 4]))
                    ve(ctx, v.tensor_sub(dnc, ssqc, dnc))
                    ve(ctx, v.tensor_scalar(dnc, dnc, 1e-6, None, ALU.max))
                    v.tensor_add(dcc, mc, t4c).then_inc(SS["ch1"], 1)
                    s.wait_ge(SS["ch1"], SS["ch1"].n)
                    s.activation(dnc, dnc, AF.Sqrt)
                    s.activation(dcc, dcc, AF.Ln).then_inc(SS["ch2"], 1)
                    v.wait_ge(SS["ch2"], SS["ch2"].n)
                    ve(ctx, v.tensor_scalar(dcc, dcc, 0.001, None, ALU.max))
                    ve(ctx, v.tensor_scalar(dnc, dnc, 1e-6, None, ALU.add))
                    ve(ctx, v.reciprocal(dnc, dnc))
                    v.tensor_mul(acol[:, st * 4:st * 4 + 4], dcc, dnc).then_inc(SS["ch1"], 1)
            def part_b(ti):
                tt = ti % 4
                if kind == "b":
                    v.wait_ge(SS["ch1"], SS["ch1"].n)
                    v.wait_ge(SS["tr"], SS["tr"].n)
                    v.tensor_copy(d0bf, d0col[:, ti:ti + 1]).then_inc(SS["ch1"], 1)
                    t.wait_ge(SS["ch1"], SS["ch1"].n)
                    transpose_col(ctx, d0bf,
                                  xnT_t[0:1, (ti % 4) * 1024:(ti % 4) * 1024 + 128])
                t.wait_ge(SS["trev"], SS["trev"].n)
                for mi, (psm, wTm) in enumerate(((ps_q, wqT), (ps_k, wkT), (ps_v, wvT))):
                    if ti >= 1:
                        t.wait_ge(SS["evA"], b_evA + 3 * ti + mi - 2)
                    for ic in range(8):
                        lt = xnT_t[:, (ti % 4) * 1024 + ic * 128:
                                   (ti % 4) * 1024 + (ic + 1) * 128]
                        t.matmul(psm[:, 0:512], lt, wTm[:, ic * 1024: ic * 1024 + 512],
                                 start=(ic == 0), stop=(ic == 7))
                        inst = t.matmul(psm[:, 512:1024], lt,
                                        wTm[:, ic * 1024 + 512:(ic + 1) * 1024],
                                        start=(ic == 0), stop=(ic == 7))
                    inst.then_inc(SS["mm"], 1)
                v.wait_ge(SS["tr"], SS["tr"].n)
                v.wait_ge(SS["mm"], SS["mm"].n - 2)
                if kind == "a":
                    cb = ctok[:, ti * 32:(ti + 1) * 32].unsqueeze(1).to_broadcast(
                        (128, 16, 32))
                    sbq = stok[:, ti * 32:(ti + 1) * 32].unsqueeze(1).to_broadcast(
                        (128, 16, 32))
                    for psrc, dstb in ((ps_q, qrot), (ps_k, krot)):
                        if psrc is ps_k:
                            v.wait_ge(SS["mm"], SS["mm"].n - 1)
                        ve(ctx, v.tensor_copy(qksb, psrc))
                        ev = qksb.rearrange("p (h q) -> p h q", q=64)[:, :, 0:32]
                        od = qksb.rearrange("p (h q) -> p h q", q=64)[:, :, 32:64]
                        r1v = rt1.rearrange("p (h q) -> p h q", q=32)
                        r2v = rt2.rearrange("p (h q) -> p h q", q=32)
                        do = dstb.rearrange("p (h q) -> p h q", q=64)
                        v.tensor_tensor(r1v, ev, cb, ALU.mult)
                        v.tensor_tensor(r2v, od, sbq, ALU.mult)
                        v.tensor_tensor(do[:, :, 0:32], r1v, r2v, ALU.subtract)
                        v.tensor_tensor(r1v, ev, sbq, ALU.mult)
                        v.tensor_tensor(r2v, od, cb, ALU.mult)
                        v.tensor_tensor(do[:, :, 32:64], r1v, r2v, ALU.add
                                        ).then_inc(SS["evA"], 1)
                    v.wait_ge(SS["mm"], SS["mm"].n)
                    v.tensor_copy(vaug[:, ti * 1040:(ti + 1) * 1040].rearrange(
                        "p (h e) -> p h e", h=16)[:, :, 0:64],
                        ps_v.rearrange("p (h e) -> p h e", h=16)).then_inc(SS["evA"], 1)
                else:
                    gl = ti % 4
                    v.wait_ge(sgl[gl], sgl_mark[ti])
                    v.scalar_tensor_tensor(
                        qrot, ps_q, acol[:, ti:ti + 1],
                        gtl[:, gl * 1024:(gl + 1) * 1024], ALU.mult, ALU.mult
                        ).then_inc(SS["evA"], 1)
                    v.wait_ge(SS["mm"], SS["mm"].n - 1)
                    v.tensor_scalar(krot, ps_k, acol[:, ti:ti + 1], None, ALU.mult
                                    ).then_inc(SS["evA"], 1)
                    v.wait_ge(SS["mm"], SS["mm"].n)
                    v.tensor_scalar(vaug[:, ti * 1040:(ti + 1) * 1040].rearrange(
                        "p (h e) -> p h e", h=16)[:, :, 0:64],
                        ps_v.rearrange("p (h e) -> p h e", h=16),
                        acol[:, ti:ti + 1], None, ALU.mult).then_inc(SS["evA"], 1)
                t.wait_ge(SS["evA"], SS["evA"].n)
                par = ti % 2
                v.wait_ge(ssp[par], ssp_mark[par])
                for rot, stage in ((qrot, qstage), (krot, kstage)):
                    srcs = [rot[:, j * 128:(j + 1) * 128] for j in range(8)]
                    dsts = []
                    for r in range(2):
                        dsts.append(stage.rearrange("p (b2 ic) -> p b2 ic", ic=128)
                                    [:, 4 * r: 4 * r + 4, :])
                    transpose_rounds(ctx, srcs, dsts)
                y.wait_ge(SS["trev"], SS["trev"].n)
                for oc in range(8):
                    y.dma_start(qsp[oc][:, ti * 128:(ti + 1) * 128],
                                qstage[:, oc * 128:(oc + 1) * 128]).then_inc(ssp[par], 16)
                    y.dma_start(ksp[oc][:, ti * 128:(ti + 1) * 128],
                                kstage[:, oc * 128:(oc + 1) * 128]).then_inc(ssp[par], 16)
                ssp_mark[par] = ssp[par].n
            def part_a2():
                src_t, ti = pend_tr[0]
                pend_tr[0] = None
                t.wait_ge(SS["xn"], xn_mark[ti])
                srcs = [src_t[:, j * 128:(j + 1) * 128] for j in range(8)]
                dsts = []
                for r in range(2):
                    dsts.append(xnT_t.rearrange("p (b4 ic) -> p b4 ic", ic=128)
                                [:, (ti % 4) * 8 + 4 * r: (ti % 4) * 8 + 4 * r + 4, :])
                transpose_rounds(ctx, srcs, dsts)
                tr_mark[ti] = SS["tr"].n

            if kind == "b":
                load_g(0)
                load_g(1)
            for ti in range(NT + 4):
                if ti < NT:
                    part_a(ti)
                j = ti - 4
                if j >= 0:
                    part_b(j)
                    if kind == "b" and j + 2 >= 2 and j + 2 < NT:
                        load_g(j + 2)
                if pend_tr[0] is not None:
                    part_a2()

            for i_ in range(2):
                y.wait_ge(ssp[i_], ssp[i_].n)
            if DBG and kind == "b":
                for nm, ap in (("d_acol", acol), ("d_d0col", d0col), ("d_ssqc", ssqc),
                               ("d_wqT", wqT), ("d_wkT", wkT), ("d_wvT", wvT),
                               ("d_xnTt", xnT_t),
                               ("d_y0c", y0c), ("d_mc", mc), ("d_t4c", t4c),
                               ("d_dnc", dnc), ("d_dcc", dcc)):
                    dd = nc.dram_tensor(nm + uid, list(ap.shape), ap.dtype,
                                        kind="ExternalOutput").ap()
                    y.dma_start(dd, ap).then_inc(SS["b2"], 16)
                y.wait_ge(SS["b2"], SS["b2"].n)
        do_barrier(ctx)

        with ExitStack() as sfk:
            psbf = lambda n, sh, dt=F32: sfk.enter_context(nc.psum_tensor(n, sh, dt))[:]
            ssbf = lambda n, sh, dt: sfk.enter_context(nc.sbuf_tensor(n, sh, dt))[:]
            kT = ssbf(uid + "kT", [128, NP * S], BF16)
            pT = ssbf(uid + "pT", [128, 2 * 2048], BF16)
            qtl = ssbf(uid + "qtl", [128, 3 * 512], BF16)
            aout = ssbf(uid + "aout", [128, 8 * 512], BF16)
            lnrow = ssbf(uid + "lnrow", [1, 1024], BF16)
            recrow = ssbf(uid + "recrow", [1, 1024], BF16)
            bcs = ssbf(uid + "bcs", [64, 1024], BF16)
            xr = ssbf(uid + "xr", [128, 2 * 1024], F32)
            xo = ssbf(uid + "xo", [128, 2 * 1024], F32)
            if kind == "b":
                zf = ssbf(uid + "zf", [128, 1024], F32)
                zw = ssbf(uid + "zw", [128, 1], F32)
                s0 = ssbf(uid + "s0", [128, 1], F32)
            ps_qk = [psbf(uid + "pqk0", [128, 1024]), psbf(uid + "pqk1", [128, 1024])]
            ps_pv = psbf(uid + "ppv", [128, 1024])
            ps_wo = psbf(uid + "pwo", [128, 1024])
            sqt = [SS["qt0"], SS["qt1"], SS["qt2"]]
            sxr = [SS["xr0"], SS["xr1"], SS["xr2"]]

            for p in range(NP):
                y.dma_start(kT[:, p * S:(p + 1) * S], ksp[p]).then_inc(SS["kld"], 16)
            t.wait_ge(SS["kld"], SS["kld"].n)

            b_qk = SS["qk"].n
            b_exp = SS["exp"].n
            b_xnew = SS["xnew"].n
            sxo_mark = [0, 0, 0]
            gg = 0
            it = 0

            def finish_pair(pp):
                t.wait_ge(SS["xnew"], SS["xnew"].n)
                t.wait_ge(SS["bc"], SS["bc"].n)
                t.wait_ge(SS["rc"], SS["rc"].n)
                t.matmul(ps_wo[0:64, 0:512], ctx.ones_bf[0:1, 0:64],
                         recrow[0:1, 0:512], start=True, stop=True)
                t.matmul(ps_wo[0:64, 512:1024], ctx.ones_bf[0:1, 0:64],
                         recrow[0:1, 512:1024], start=True, stop=True).then_inc(SS["rep"], 1)
                v.wait_ge(SS["rep"], SS["rep"].n)
                ve(ctx, v.tensor_copy(bcs, ps_wo[0:64, 0:1024]))
                v.nop().then_inc(SS["bc"], 1)
                if pp == 0:
                    v.wait_ge(SS["wo"], SS["wo"].n)
                v.tensor_tensor(aout[0:64, pp * 512:(pp + 1) * 512],
                                ps_pv[0:64, 0:512], bcs[0:64, 0:512], ALU.mult)
                v.tensor_tensor(aout[64:128, pp * 512:(pp + 1) * 512],
                                ps_pv[0:64, 512:1024], bcs[0:64, 512:1024], ALU.mult
                                ).then_inc(SS["ao"], 1)

            def emit_wo_tt(wqt, tt):
                ti = wqt * 4 + tt
                if tt == 0:
                    t.wait_ge(SS["ao"], SS["ao"].n)
                    t.wait_ge(SS["bc"], SS["bc"].n)
                else:
                    t.wait_ge(SS["xnew"], SS["xnew"].n)
                for cc in range(8):
                    t.matmul(ps_wo[:, 0:512],
                             aout[:, cc * 512 + tt * 128: cc * 512 + (tt + 1) * 128],
                             woT[:, cc * 1024: cc * 1024 + 512],
                             start=(cc == 0), stop=(cc == 7))
                    inst = t.matmul(ps_wo[:, 512:1024],
                                    aout[:, cc * 512 + tt * 128: cc * 512 + (tt + 1) * 128],
                                    woT[:, cc * 1024 + 512: (cc + 1) * 1024],
                                    start=(cc == 0), stop=(cc == 7))
                inst.then_inc(SS["wo"], 1)
                slr = ti % 2
                if ti >= 2:
                    y.wait_ge(SS["xnew"], b_xnew + ti - 1)
                y.dma_start(xr[:, slr * 1024:(slr + 1) * 1024], xs[ti]).then_inc(sxr[slr], 16)
                v.wait_ge(SS["wo"], SS["wo"].n)
                v.wait_ge(sxr[slr], sxr[slr].n)
                slo = ti % 2
                if ti >= 2:
                    v.wait_ge(sxo[slo], sxo_mark[slo])
                xov = xo[:, slo * 1024:(slo + 1) * 1024]
                if kind == "a":
                    v.tensor_tensor(xov, ps_wo, xr[:, slr * 1024:(slr + 1) * 1024],
                                    ALU.add).then_inc(SS["xnew"], 1)
                else:
                    ve(ctx, v.tensor_scalar(zf, ps_wo, 0.1, None, ALU.mult))
                    ve(ctx, v.tensor_scalar(zf[:, 0:1], zf[:, 0:1],
                                            float(np.sqrt(np.float32(1.0) + np.float32(1e-6))),
                                            None, ALU.add))
                    ve(ctx, v.tensor_scalar(zf, zf, 8.0, -8.0, ALU.min, ALU.max))
                    ve(ctx, ssq_via_bn(ctx, zf, s0[:, 0:1]))
                    v.tensor_scalar(zw, s0, 1.0 + 1e-6, None, ALU.add).then_inc(SS["tl1"], 1)
                    s.wait_ge(SS["tl1"], SS["tl1"].n)
                    se(ctx, s.activation(zw, zw, AF.Ln))
                    s.activation(zw, zw, AF.Exp, scale=0.5).then_inc(SS["tl2"], 1)
                    v.wait_ge(SS["tl2"], SS["tl2"].n)
                    ve(ctx, v.tensor_copy(zf[:, 0:1], zw))
                    v.tensor_tensor(xov, zf, xr[:, slr * 1024:(slr + 1) * 1024],
                                    ALU.add).then_inc(SS["xnew"], 1)
                y.wait_ge(SS["xnew"], SS["xnew"].n)
                y.dma_start(ds[ti], xov).then_inc(sxo[slo], 16)
                sxo_mark[slo] = sxo[slo].n

            pend = None
            pend_wo = None
            for qt in range(NST):
                for p in range(NP):
                    it += 1
                    qslot = (it - 1) % 3
                    if it >= 4:
                        y.wait_ge(SS["qk"], b_qk + 32 * (it - 3))
                    y.dma_start(qtl[:, qslot * 512:(qslot + 1) * 512],
                                qsp[p][:, qt * 512:(qt + 1) * 512]).then_inc(sqt[qslot], 16)
                    t.wait_ge(sqt[qslot], sqt[qslot].n)
                    gg0 = gg
                    for kc in range(NT):
                        gg += 1
                        if gg > 2:
                            t.wait_ge(SS["exp"], b_exp + gg - 2)
                        k0 = p * S + kc * 128
                        cA = (gg - 1) % 2
                        t.matmul(ps_qk[cA][:, 0:512], kT[0:64, k0:k0 + 128],
                                 qtl[0:64, qslot * 512:(qslot + 1) * 512],
                                 start=True, stop=True, tile_position=(0, 0))
                        t.matmul(ps_qk[cA][:, 512:1024], kT[64:128, k0:k0 + 128],
                                 qtl[64:128, qslot * 512:(qslot + 1) * 512],
                                 start=True, stop=True, tile_position=(64, 0)
                                 ).then_inc(SS["qk"], 1)
                        s.wait_ge(SS["qk"], SS["qk"].n)
                        pb = ((gg - 1) % 4) * 1024
                        if kind == "a":
                            s.activation(pT[:, pb:pb + 1024], ps_qk[cA], AF.Exp, scale=0.125,
                                         bias=ctx.consts[:, C_EXPSHIFT:C_EXPSHIFT + 1]
                                         ).then_inc(SS["exp"], 1)
                        else:
                            s.activation(pT[:, pb:pb + 1024], ps_qk[cA], AF.Exp,
                                         scale=0.125).then_inc(SS["exp"], 1)
                        if kc == 1 and pend is not None:
                            finish_pair(pend)
                            pend = None
                        if pend_wo is not None and kc in (2, 4, 6, 8):
                            emit_wo_tt(pend_wo, (kc - 2) // 2)
                            if kc == 8:
                                pend_wo = None
                        if kc >= 3:
                            if kc == 3:
                                t.wait_ge(SS["ao"], SS["ao"].n)
                            pb2 = ((gg - 4) % 4) * 1024
                            for h2 in range(2):
                                hh = 2 * p + h2
                                t.matmul(ps_pv[0:65, h2 * 512:(h2 + 1) * 512],
                                         vaug[:, (kc - 3) * 1040 + hh * 65:
                                              (kc - 3) * 1040 + (hh + 1) * 65],
                                         pT[:, pb2 + h2 * 512: pb2 + (h2 + 1) * 512],
                                         start=(kc == 3), stop=False)
                    for kc in (NT - 3, NT - 2, NT - 1):
                        t.wait_ge(SS["exp"], b_exp + gg0 + kc + 1)
                        pb2 = ((gg0 + kc) % 4) * 1024
                        for h2 in range(2):
                            hh = 2 * p + h2
                            inst = t.matmul(ps_pv[0:65, h2 * 512:(h2 + 1) * 512],
                                            vaug[:, kc * 1040 + hh * 65:
                                                 kc * 1040 + (hh + 1) * 65],
                                            pT[:, pb2 + h2 * 512: pb2 + (h2 + 1) * 512],
                                            start=False, stop=(kc == NT - 1))
                    inst.then_inc(SS["pv"], 1)
                    s.wait_ge(SS["pv"], SS["pv"].n)
                    s.wait_ge(SS["rep"], SS["rep"].n)
                    se(ctx, s.activation(lnrow, ps_pv[64:65, 0:1024], AF.Ln))
                    s.activation(recrow, lnrow, AF.Exp, scale=-1.0).then_inc(SS["rc"], 1)
                    pend = p
                pend_wo = qt
            finish_pair(pend)
            pend = None
            for tt in range(4):
                emit_wo_tt(NST - 1, tt)
            for sl in range(2):
                y.wait_ge(sxo[sl], sxo[sl].n)
            if DBG:
                for nm, ap in (("d_vaug", vaug), ("d_kT", kT)):
                    dd = nc.dram_tensor(nm + uid, list(ap.shape), ap.dtype,
                                        kind="ExternalOutput").ap()
                    y.dma_start(dd, ap).then_inc(SS["b2"], 16)
                y.wait_ge(SS["b2"], SS["b2"].n)


def host_consts():
    c = np.zeros((128, 8), dtype=np.float32)
    c[:, C_EXPSHIFT] = -16.0
    c[:, C_LNEPS] = 1e-5
    c[:, C_1PEPS] = np.float32(1.0) + np.float32(1e-6)
    c[:, C_NEG1] = -1.0
    return c


def host_rope():
    inv = 1.0 / 10000.0 ** (np.arange(0, HD, 2, dtype=np.float32) / HD)
    tpos = np.arange(S, dtype=np.float32)
    ang = np.outer(tpos, inv)
    return np.cos(ang).astype(np.float32), np.sin(ang).astype(np.float32)


def make_inmaps(inputs):
    cos, sin = host_rope()
    base = dict(
        wq=inputs["L0_attn_a_wq"], wk=inputs["L0_attn_a_wk"],
        wv=inputs["L0_attn_a_wv"], wo=inputs["L0_attn_a_wo"],
        fw1=inputs["L0_ffn_w1"], fb1=inputs["L0_ffn_b1"],
        fw2=inputs["L0_ffn_w2"], fb2=inputs["L0_ffn_b2"],
        mW1=inputs["L0_attn_b_W1"], mb1=inputs["L0_attn_b_b1"],
        fb1c=np.ascontiguousarray(np.asarray(inputs["L0_ffn_b1"], dtype=np.float32).reshape(32, 128).T),
        mb1c=np.ascontiguousarray(np.asarray(inputs["L0_attn_b_b1"], dtype=np.float32).reshape(16, 128).T),
        mW2=inputs["L0_attn_b_W2"], mb2=inputs["L0_attn_b_b2"],
        bWq=inputs["L0_attn_b_Wq"], bWk=inputs["L0_attn_b_Wk"],
        bWv=inputs["L0_attn_b_Wv"], bWo=inputs["L0_attn_b_Wo"],
        ident=np.eye(128, dtype=ml_dtypes.bfloat16),
        identf=np.eye(128, dtype=np.float32),
        ones_bf=np.ones((1, 128), dtype=ml_dtypes.bfloat16),
        ones_f=np.ones((1, 128), dtype=np.float32),
        consts=host_consts(), ctok=cos, stok=sin,
    )
    base = {k: np.ascontiguousarray(np.asarray(v)) for k, v in base.items()}
    x = np.ascontiguousarray(np.asarray(inputs["L0_attn_a_x"], dtype=np.float32))
    maps = []
    for b in range(8):
        m = dict(base)
        m["x"] = np.ascontiguousarray(x[b])
        maps.append(m)
    return maps


def kernel(**inputs):
    nc = build(4)
    maps = make_inmaps(inputs)
    res = run_bass_kernel_spmd(nc, maps, core_ids=list(range(8)))
    return np.stack([res.results[b]["out"] for b in range(8)], axis=0)


# version 11
# speedup vs baseline: 1.1249x; 1.1249x over previous
import numpy as np
import ml_dtypes
from contextlib import ExitStack
import concourse.bass as bass
import concourse.mybir as mybir
from concourse.bass_utils import run_bass_kernel_spmd

F32 = mybir.dt.float32
BF16 = mybir.dt.bfloat16
AF = mybir.ActivationFunctionType
ALU = mybir.AluOpType

S, D, H, HD = 4096, 1024, 16, 64
NT = S // 128
NST = S // 512
NP = H // 2
FF = 4096
MH = 2048

ENGS = ("t", "v", "s", "y", "g")
DBG = False

SEM_NAMES = [
    "bar", "pre", "tr", "trev",
    "xl0", "xl1", "xl2", "xr0", "xr1", "xr2", "xo0", "xo1", "xo2",
    "qt0", "qt1", "qt2", "sp0", "sp1", "gl0", "gl1", "gl2", "gl3", "gst0", "gst1",
    "wl0", "wl1", "wl2", "wd",
    "ln1", "ln2", "ln3", "xn", "pp", "mm", "evA", "xnew",
    "h1", "act1", "mm2", "b2", "bin1", "bin2",
    "qk", "exp", "pv", "rc", "rep", "bc", "ao", "wo", "tl1", "tl2", "kld",
    "ch1", "ch2", "g1", "g2", "g3", "vv", "sv",
]


class Sem:
    __slots__ = ("h", "n", "name")

    def __init__(self, h, name):
        self.h, self.n, self.name = h, 0, name


class RecInst:
    __slots__ = ("items", "idx")

    def __init__(self, items, idx):
        self.items, self.idx = items, idx

    def then_inc(self, sem, k):
        self.items[self.idx][3].append((sem, k))
        sem.n += k
        return self


class Rec:
    def __init__(self, items):
        self.__dict__["items"] = items

    def wait_ge(self, sem, v):
        assert v <= sem.n, f"wait {v} > expected {sem.n} on {sem.name}"
        if v > 0:
            self.items.append(("wait", sem, v, []))

    def __getattr__(self, name):
        items = self.items

        def call(*args, **kw):
            items.append(("call", name, (args, kw), []))
            return RecInst(items, len(items) - 1)

        return call


def replay(eng, items):
    for idx, (kind, a, b, incs) in enumerate(items):
        if kind == "wait":
            eng.wait_ge(a.h, b)
        else:
            try:
                inst = getattr(eng, a)(*b[0], **b[1])
            except Exception as e:
                raise RuntimeError(f"replay fail at item {idx}: {a} args={[getattr(x, 'shape', x) for x in b[0]]} kw={list(b[1])}") from e
            for sem, k in incs:
                inst = inst.then_inc(sem.h, k)


class Ctx:
    pass


C_EXPSHIFT, C_LNEPS, C_1PEPS, C_NEG1 = 0, 1, 2, 3



def ve(ctx, inst):
    inst.then_inc(ctx.S["vv"], 1)
    ctx.v.wait_ge(ctx.S["vv"], ctx.S["vv"].n)


def se(ctx, inst):
    inst.then_inc(ctx.S["sv"], 1)
    ctx.s.wait_ge(ctx.S["sv"], ctx.S["sv"].n)


def ln_tile(ctx, xap, xn_out):
    v, s, SS = ctx.v, ctx.s, ctx.S
    bn6, mv, sd, rstd = ctx.bn6, ctx.mv, ctx.sd, ctx.rstd
    v.bn_stats(bn6[:, 0:6], xap[:, 0:512])
    v.bn_stats(bn6[:, 6:12], xap[:, 512:1024]).then_inc(SS["ln1"], 1)
    v.wait_ge(SS["ln1"], SS["ln1"].n)
    v.bn_aggr(mv[:, 0:2], bn6[:, 0:12]).then_inc(SS["ln2"], 1)
    s.wait_ge(SS["ln2"], SS["ln2"].n)
    s.activation(sd[:, 0:1], mv[:, 1:2], AF.Sqrt,
                 bias=ctx.consts[:, C_LNEPS:C_LNEPS + 1]).then_inc(SS["ln3"], 1)
    v.wait_ge(SS["ln3"], SS["ln3"].n)
    ve(ctx, v.reciprocal(rstd[:, 0:1], sd[:, 0:1]))
    return v.tensor_scalar(xn_out, xap, mv[:, 0:1], rstd[:, 0:1], ALU.subtract, ALU.mult)



def ssq_via_bn(ctx, sp, out_col):
    v, SS = ctx.v, ctx.S
    v.bn_stats(ctx.bn6[:, 0:6], sp[:, 1:513])
    v.bn_stats(ctx.bn6[:, 6:12], sp[:, 513:1024]).then_inc(SS["ln1"], 1)
    v.wait_ge(SS["ln1"], SS["ln1"].n)
    ve(ctx, v.bn_aggr(ctx.mv[:, 0:2], ctx.bn6[:, 0:12]))
    ve(ctx, v.tensor_tensor(ctx.sd[:, 0:1], ctx.mv[:, 0:1], ctx.mv[:, 0:1], ALU.mult))
    ve(ctx, v.tensor_tensor(ctx.sd[:, 0:1], ctx.mv[:, 1:2], ctx.sd[:, 0:1], ALU.add))
    return v.tensor_scalar(out_col, ctx.sd[:, 0:1], 1023.0, None, ALU.mult)


def do_barrier(ctx):
    ctx.bar_round += 1
    tgt = 5 * ctx.bar_round
    for k in ENGS:
        getattr(ctx, k).nop().then_inc(ctx.S["bar"], 1)
    assert ctx.S["bar"].n == tgt
    for k in ENGS:
        getattr(ctx, k).wait_ge(ctx.S["bar"], tgt)


def transpose_rounds(ctx, srcs, dsts3d):
    t, v, SS = ctx.t, ctx.v, ctx.S
    n = len(srcs)
    for r in range((n + 3) // 4):
        bank = ctx.ps_tA if (ctx.tr_round % 2 == 0) else ctx.ps_tB
        if ctx.tr_round >= 2:
            t.wait_ge(SS["trev"], ctx.tr_round - 1)
        k = min(4, n - 4 * r)
        for j in range(k):
            inst = t.transpose(bank[:, j * 128:(j + 1) * 128], srcs[4 * r + j], ctx.ident)
        inst.then_inc(SS["tr"], 1)
        v.wait_ge(SS["tr"], SS["tr"].n)
        v.tensor_copy(dsts3d[r],
                      bank[:, 0:k * 128].rearrange("p (a b) -> p a b", b=128)
                      ).then_inc(SS["trev"], 1)
        ctx.tr_round += 1


def transpose_col(ctx, src_col_bf, dst_row):
    t, v, SS = ctx.t, ctx.v, ctx.S
    bank = ctx.ps_tA if (ctx.tr_round % 2 == 0) else ctx.ps_tB
    if ctx.tr_round >= 2:
        t.wait_ge(SS["trev"], ctx.tr_round - 1)
    t.transpose(bank[0:1, 0:128], src_col_bf, ctx.ident).then_inc(SS["tr"], 1)
    v.wait_ge(SS["tr"], SS["tr"].n)
    v.tensor_copy(dst_row, bank[0:1, 0:128]).then_inc(SS["trev"], 1)
    ctx.tr_round += 1


def wT_prepare(ctx, w_d, nch, cw, dstT, TW, perm=False):
    t, v, g, SS = ctx.t, ctx.v, ctx.g, ctx.S
    wst = ctx.wst
    swl = [SS["wl0"], SS["wl1"]]
    b_wd = SS["wd"].n
    G = max(1, 4096 // cw)
    nb = nch // G
    g.wait_ge(SS["trev"], SS["trev"].n)
    for b in range(nb):
        sl = b % 2
        if b >= 2:
            g.wait_ge(SS["wd"], b_wd + (b - 1) * G)
        if perm:
            src6 = w_d.rearrange("(bb c h r b2) i -> bb h b2 r c i",
                                 bb=nb, c=G, h=2, r=32, b2=2)
            for hh in range(2):
                for b2 in range(2):
                    base = hh * 64 + b2 * 32
                    g.dma_start(wst[base:base + 32, sl * 4096: sl * 4096 + G * cw]
                                .rearrange("p (c i) -> p c i", i=cw),
                                src6[b, hh, b2]).then_inc(swl[sl], 16)
        else:
            srcap = w_d.rearrange("(bb c p) i -> bb p c i", bb=nb, c=G, p=128)[b]
            g.dma_start(wst[:, sl * 4096: sl * 4096 + G * cw]
                        .rearrange("p (c i) -> p c i", i=cw), srcap).then_inc(swl[sl], 16)
        t.wait_ge(swl[sl], swl[sl].n)
        for ci in range(G):
            c = b * G + ci
            srcs = [wst[:, sl * 4096 + ci * cw + j * 128:
                        sl * 4096 + ci * cw + (j + 1) * 128]
                    for j in range(cw // 128)]
            ntr = cw // 128
            dsts = []
            for r in range((ntr + 3) // 4):
                k = min(4, ntr - 4 * r)
                dsts.append(dstT.rearrange("p (q c2) -> p q c2", c2=TW)
                            [:, 4 * r:4 * r + k, c * 128:(c + 1) * 128])
            transpose_rounds(ctx, srcs, dsts)
            v.nop().then_inc(SS["wd"], 1)



def load_bias_cols(ctx, src_1d, nch, dst_col, stg, ps_f32):
    t, v, y, SS = ctx.t, ctx.v, ctx.y, ctx.S
    y.dma_start(stg[0:nch, :], src_1d.rearrange("(c p) -> c p", p=128)).then_inc(SS["b2"], 16)
    t.wait_ge(SS["b2"], SS["b2"].n)
    t.transpose(ps_f32[:, 0:nch], stg[0:nch, :], ctx.identf[0:nch, 0:nch]
                ).then_inc(SS["b2"], 1)
    v.wait_ge(SS["b2"], SS["b2"].n)
    v.tensor_copy(dst_col, ps_f32[:, 0:nch]).then_inc(SS["b2"], 1)
    t.wait_ge(SS["b2"], SS["b2"].n)


def build(nph=4, phase_list=("a", "f", "b", "f")):
    nc = bass.Bass("TRN2", target_bir_lowering=False)
    ctx = Ctx()
    ctx.nc = nc
    ctx.work = {k: [] for k in ENGS}
    for k in ENGS:
        setattr(ctx, k, Rec(ctx.work[k]))
    ctx.stack = ExitStack()
    ctx.bar_round = 0
    ctx.tr_round = 0
    ctx.S = {}
    for nm in SEM_NAMES:
        h = ctx.stack.enter_context(nc.semaphore(nm))
        ctx.S[nm] = Sem(h, nm)

    dri = lambda n, sh, dt=F32: nc.dram_tensor(n, sh, dt, kind="ExternalInput").ap()
    x_in = dri("x", [S, D])
    w = {k: dri(k, [D, D]) for k in ("wq", "wk", "wv", "wo", "bWq", "bWk", "bWv", "bWo")}
    w["fw1"] = dri("fw1", [FF, D]); w["fb1"] = dri("fb1", [FF])
    w["fw2"] = dri("fw2", [D, FF]); w["fb2"] = dri("fb2", [D])
    w["mW1"] = dri("mW1", [MH, D]); w["mb1"] = dri("mb1", [MH])
    w["mW2"] = dri("mW2", [D, MH]); w["mb2"] = dri("mb2", [D])
    w["fb1c"] = dri("fb1c", [128, 32]); w["mb1c"] = dri("mb1c", [128, 16])
    ident_d = dri("ident", [128, 128], BF16)
    identf_d = dri("identf", [128, 128])
    ones_bf_d = dri("ones_bf", [1, 128], BF16)
    ones_f_d = dri("ones_f", [1, 128])
    consts_d = dri("consts", [128, 8])
    ctok_d = dri("ctok", [S, 32])
    stok_d = dri("stok", [S, 32])
    out_d = nc.dram_tensor("out", [S, D], F32, kind="ExternalOutput").ap()
    xA = nc.dram_tensor("xA", [S, D], F32, kind="Internal").ap()
    xB = nc.dram_tensor("xB", [S, D], F32, kind="Internal").ap()
    spkind = "ExternalOutput" if DBG else "Internal"
    qsp = nc.dram_tensor("qsp", [NP, 128, S], BF16, kind=spkind).ap()
    ksp = nc.dram_tensor("ksp", [NP, 128, S], BF16, kind=spkind).ap()
    gsp = nc.dram_tensor("gsp", [NT, 128, D], BF16, kind=spkind).ap()
    bw1T_d = nc.dram_tensor("bw1T_d", [128, 8 * FF], BF16, kind="Internal").ap()
    bw2T_d = nc.dram_tensor("bw2T_d", [128, 32 * D], BF16, kind="Internal").ap()

    sb = lambda n, sh, dt: ctx.stack.enter_context(nc.sbuf_tensor(n, sh, dt))
    ctx.ident = sb("identb", [128, 128], BF16)[:]
    ctx.identf = sb("identf_sb", [128, 128], F32)[:]
    ctx.ones_bf = sb("onesb", [1, 128], BF16)[:]
    ctx.ones_f = sb("onesf", [1, 128], F32)[:]
    ctx.consts = sb("consts_sb", [128, 8], F32)[:]
    ctx.bn6 = sb("bn6", [128, 12], F32)[:]
    ctx.mv = sb("mv", [128, 2], F32)[:]
    ctx.sd = sb("sd", [128, 1], F32)[:]
    ctx.rstd = sb("rstd", [128, 1], F32)[:]
    ctx.wst = sb("wst", [128, 2 * 4096], BF16)[:]

    SS = ctx.S
    y = ctx.y
    y.dma_start(ctx.ident, ident_d).then_inc(SS["pre"], 16)
    y.dma_start(ctx.identf, identf_d).then_inc(SS["pre"], 16)
    y.dma_start(ctx.ones_bf, ones_bf_d).then_inc(SS["pre"], 16)
    y.dma_start(ctx.ones_f, ones_f_d).then_inc(SS["pre"], 16)
    y.dma_start(ctx.consts, consts_d).then_inc(SS["pre"], 16)
    ctx.ctok_d, ctx.stok_d = ctok_d, stok_d
    for k in ENGS:
        getattr(ctx, k).wait_ge(SS["pre"], 80)

    bufs = [x_in]
    for i in range(nph - 1):
        bufs.append(xA if i % 2 == 0 else xB)
    bufs.append(out_d)

    prepped = False
    for i, ph in enumerate(phase_list[:nph]):
        src, dst = bufs[i], bufs[i + 1]
        if ph in ("a", "b"):
            ph_attn(ctx, src, dst, w, qsp, ksp, gsp, kind=ph)
        else:
            ph_ffn(ctx, src, dst, w, bw1T_d, bw2T_d, prep=not prepped)
            prepped = True
        do_barrier(ctx)

    with ctx.stack:
        with nc.Block() as block:
            @block.tensor
            def _(e):
                replay(e, ctx.work["t"])

            @block.vector
            def _(e):
                replay(e, ctx.work["v"])

            @block.scalar
            def _(e):
                replay(e, ctx.work["s"])

            @block.sync
            def _(e):
                replay(e, ctx.work["y"])

            @block.gpsimd
            def _(e):
                replay(e, ctx.work["g"])
    return nc


def ph_ffn(ctx, src, dst, w, bw1T_d, bw2T_d, prep):
    nc = ctx.nc
    t, v, s, y, g, SS = ctx.t, ctx.v, ctx.s, ctx.y, ctx.g, ctx.S
    STW = 256
    NSTF = S // STW
    with ExitStack() as ph:
        psb = lambda n, sh, dt=F32: ph.enter_context(nc.psum_tensor(n, sh, dt))[:]
        ssb = lambda n, sh, dt: ph.enter_context(nc.sbuf_tensor(n, sh, dt))[:]
        uid = f"F{ctx.bar_round}_"

        bw1T = ssb(uid + "bw1T", [128, 8 * FF], BF16)
        bw2T = ssb(uid + "bw2T", [128, 32 * D], BF16)
        b1col = ssb(uid + "b1col", [128, 32], F32)
        b2bc = ssb(uid + "b2bc", [128, D], F32)

        ps_h = [psb(uid + "psh0", [128, 512]), psb(uid + "psh1", [128, 512])]
        ps_o = [psb(uid + "pso0", [128, 1024]), psb(uid + "pso1", [128, 1024])]
        ctx.ps_tA = psb(uid + "ptA", [128, 1024], BF16)
        ctx.ps_tB = psb(uid + "ptB", [128, 1024], BF16)

        sxl = [SS["xl0"], SS["xl1"], SS["xl2"]]
        sxr = [SS["xr0"], SS["xr1"], SS["xr2"]]
        sxo = [SS["xo0"], SS["xo1"], SS["xo2"]]

        if prep:
            with ExitStack() as pp:
                bwc = pp.enter_context(nc.sbuf_tensor(uid + "bwc", [128, 2 * 4096], BF16))[:]
                alpha = pp.enter_context(nc.sbuf_tensor(uid + "alpha", [128, 1], F32))[:]
                swl = [SS["wl0"], SS["wl1"]]
                b_bin1, b_bin2 = SS["bin1"].n, SS["bin2"].n
                chunk_trev = []
                slot_i = 0
                g.wait_ge(SS["trev"], SS["trev"].n)
                for name, nch, cw, dstT, TW in (("fw1", 32, 1024, bw1T, FF),
                                                ("fw2", 8, 4096, bw2T, D)):
                    for c in range(nch):
                        sl = slot_i % 2
                        if slot_i >= 2:
                            g.wait_ge(SS["bin1"], b_bin1 + slot_i - 1)
                        g.dma_start(ctx.wst[:, sl * 4096: sl * 4096 + cw],
                                    w[name].rearrange("(c p) i -> c p i", p=128)[c]
                                    ).then_inc(swl[sl], 16)
                        s.wait_ge(swl[sl], swl[sl].n)
                        bsl = slot_i % 2
                        if slot_i >= 2:
                            s.wait_ge(SS["trev"], chunk_trev[slot_i - 2])
                        s.activation(bwc[:, bsl * 4096: bsl * 4096 + cw],
                                     ctx.wst[:, sl * 4096: sl * 4096 + cw],
                                     AF.Abs, accum_out=alpha[:, 0:1])
                        s.activation(bwc[:, bsl * 4096: bsl * 4096 + cw],
                                     ctx.wst[:, sl * 4096: sl * 4096 + cw],
                                     AF.Sign).then_inc(SS["bin1"], 1)
                        v.wait_ge(SS["bin1"], SS["bin1"].n)
                        v.tensor_scalar(bwc[:, bsl * 4096: bsl * 4096 + cw],
                                        bwc[:, bsl * 4096: bsl * 4096 + cw],
                                        alpha[:, 0:1], 1.0 / cw,
                                        ALU.mult, ALU.mult).then_inc(SS["bin2"], 1)
                        t.wait_ge(SS["bin2"], SS["bin2"].n)
                        srcs = [bwc[:, bsl * 4096 + j * 128: bsl * 4096 + (j + 1) * 128]
                                for j in range(cw // 128)]
                        ntr = cw // 128
                        dsts = []
                        for r in range((ntr + 3) // 4):
                            kk = min(4, ntr - 4 * r)
                            dsts.append(dstT.rearrange("p (q c2) -> p q c2", c2=TW)
                                        [:, 4 * r:4 * r + kk, c * 128:(c + 1) * 128])
                        transpose_rounds(ctx, srcs, dsts)
                        chunk_trev.append(SS["trev"].n)
                        slot_i += 1
                y.wait_ge(SS["trev"], SS["trev"].n)
                y.dma_start(bw1T_d, bw1T).then_inc(SS["b2"], 16)
                y.dma_start(bw2T_d, bw2T).then_inc(SS["b2"], 16)
        else:
            y.dma_start(bw1T, bw1T_d).then_inc(SS["b2"], 16)
            y.dma_start(bw2T, bw2T_d).then_inc(SS["b2"], 16)
        y.dma_start(b1col, w["fb1c"]).then_inc(SS["b2"], 16)
        y.dma_start(b2bc[0:1, :], w["fb2"].rearrange("(o d) -> o d", o=1)).then_inc(SS["b2"], 16)
        t.wait_ge(SS["b2"], SS["b2"].n)
        t.matmul(ps_o[0][:, 0:512], ctx.ones_f, b2bc[0:1, 0:512], start=True, stop=True)
        t.matmul(ps_o[0][:, 512:1024], ctx.ones_f, b2bc[0:1, 512:1024],
                 start=True, stop=True).then_inc(SS["b2"], 1)
        v.wait_ge(SS["b2"], SS["b2"].n)
        v.tensor_copy(b2bc, ps_o[0]).then_inc(SS["b2"], 1)
        t.wait_ge(SS["b2"], SS["b2"].n)

        h1T = ssb(uid + "h1T", [128, 32 * STW], BF16)
        xnT = ssb(uid + "xnT", [128, 8 * STW], BF16)
        xn_b = ssb(uid + "xn", [128, 4 * 1024], BF16)
        xl = ssb(uid + "xl", [128, 2 * 1024], F32)
        xr = ssb(uid + "xr", [128, 2 * 1024], F32)
        xo = ssb(uid + "xo", [128, 2 * 1024], F32)

        xs = src.rearrange("(n p) d -> n p d", p=128)
        ds = dst.rearrange("(n p) d -> n p d", p=128)
        b_xn = SS["xn"].n
        b_xnew = SS["xnew"].n
        b_gelu = SS["act1"].n
        xn_tr_mark = []
        sxo_mark = [0, 0, 0]
        y.wait_ge(SS["trev"], SS["trev"].n)

        for st in range(NSTF):
            for tt in range(2):
                ti = st * 2 + tt
                sl = ti % 2
                if ti >= 2:
                    y.wait_ge(SS["xn"], b_xn + ti - 1)
                y.dma_start(xl[:, sl * 1024:(sl + 1) * 1024], xs[ti]).then_inc(sxl[sl], 16)
                v.wait_ge(sxl[sl], sxl[sl].n)
                if ti >= 4:
                    v.wait_ge(SS["tr"], xn_tr_mark[ti - 4])
                v.wait_ge(SS["h1"], SS["h1"].n)
                xb = xn_b[:, (ti % 4) * 1024:(ti % 4 + 1) * 1024]
                ln_tile(ctx, xl[:, sl * 1024:(sl + 1) * 1024], xb).then_inc(SS["xn"], 1)
                t.wait_ge(SS["xn"], SS["xn"].n)
                srcs = [xb[:, j * 128:(j + 1) * 128] for j in range(8)]
                dsts = []
                for r in range(2):
                    dsts.append(xnT.rearrange("p (ic q) -> p ic q", q=STW)
                                [:, 4 * r:4 * r + 4, tt * 128:(tt + 1) * 128])
                transpose_rounds(ctx, srcs, dsts)
                xn_tr_mark.append(SS["tr"].n)
            t.wait_ge(SS["trev"], SS["trev"].n)
            for hc in range(32):
                bank = hc % 2
                if SS["act1"].n - b_gelu >= 2:
                    t.wait_ge(SS["act1"], SS["act1"].n - 1)
                for ic in range(8):
                    inst = t.matmul(ps_h[bank][:, 0:STW],
                                    bw1T[:, ic * FF + hc * 128: ic * FF + (hc + 1) * 128],
                                    xnT[:, ic * STW:(ic + 1) * STW],
                                    start=(ic == 0), stop=(ic == 7))
                inst.then_inc(SS["h1"], 1)
                s.wait_ge(SS["h1"], SS["h1"].n)
                s.activation(h1T[:, hc * STW:(hc + 1) * STW], ps_h[bank][:, 0:STW], AF.Gelu,
                             bias=b1col[:, hc:hc + 1]).then_inc(SS["act1"], 1)
            t.wait_ge(SS["act1"], SS["act1"].n)
            for tt in range(2):
                ti = st * 2 + tt
                bank = tt % 2
                if ti >= 2:
                    t.wait_ge(SS["xnew"], b_xnew + ti - 1)
                for hc in range(32):
                    t.matmul(ps_o[bank][:, 0:512],
                             h1T[:, hc * STW + tt * 128: hc * STW + (tt + 1) * 128],
                             bw2T[:, hc * D: hc * D + 512],
                             start=(hc == 0), stop=(hc == 31))
                    inst = t.matmul(ps_o[bank][:, 512:1024],
                                    h1T[:, hc * STW + tt * 128: hc * STW + (tt + 1) * 128],
                                    bw2T[:, hc * D + 512: hc * D + 1024],
                                    start=(hc == 0), stop=(hc == 31))
                inst.then_inc(SS["mm2"], 1)
                slr = ti % 2
                if ti >= 2:
                    y.wait_ge(SS["xnew"], b_xnew + ti - 1)
                y.dma_start(xr[:, slr * 1024:(slr + 1) * 1024], xs[ti]).then_inc(sxr[slr], 16)
                v.wait_ge(SS["mm2"], SS["mm2"].n)
                v.wait_ge(sxr[slr], sxr[slr].n)
                slo = ti % 2
                if ti >= 2:
                    v.wait_ge(sxo[slo], sxo_mark[slo])
                xov = xo[:, slo * 1024:(slo + 1) * 1024]
                v.tensor_tensor(xov, ps_o[bank], b2bc, ALU.add)
                v.tensor_tensor(xov, xov, xr[:, slr * 1024:(slr + 1) * 1024], ALU.add
                                ).then_inc(SS["xnew"], 1)
                g.wait_ge(SS["xnew"], SS["xnew"].n)
                g.dma_start(ds[ti], xov).then_inc(sxo[slo], 16)
                sxo_mark[slo] = sxo[slo].n
        for sl in range(2):
            y.wait_ge(sxo[sl], sxo[sl].n)
        if DBG:
            dmp = {"d_bw1T": bw1T, "d_bw2T": bw2T, "d_b1col": b1col,
                   "d_b2bc": b2bc, "d_h1T": h1T, "d_xnT": xnT}
            for nm, ap in dmp.items():
                dd = nc.dram_tensor(nm, list(ap.shape),
                                    ap.dtype, kind="ExternalOutput").ap()
                y.wait_ge(SS["xnew"], SS["xnew"].n)
                y.dma_start(dd, ap).then_inc(SS["b2"], 16)
            y.wait_ge(SS["b2"], SS["b2"].n)


def ph_attn(ctx, src, dst, w, qsp, ksp, gsp, kind):
    nc = ctx.nc
    t, v, s, y, g, SS = ctx.t, ctx.v, ctx.s, ctx.y, ctx.g, ctx.S
    xs = src.rearrange("(n p) d -> n p d", p=128)
    ds = dst.rearrange("(n p) d -> n p d", p=128)
    uid = f"A{ctx.bar_round}{kind}_"
    sxl = [SS["xl0"], SS["xl1"], SS["xl2"]]
    sxo = [SS["xo0"], SS["xo1"], SS["xo2"]]

    if kind == "b":
        with ExitStack() as smk:
            psbm = lambda n, sh, dt=F32: smk.enter_context(nc.psum_tensor(n, sh, dt))[:]
            ssbm = lambda n, sh, dt: smk.enter_context(nc.sbuf_tensor(n, sh, dt))[:]
            w1T = ssbm(uid + "w1T", [128, 8 * MH], BF16)
            w2T = ssbm(uid + "w2T", [128, 16 * D], BF16)
            b1c = ssbm(uid + "b1c", [128, 16], F32)
            b2bc = ssbm(uid + "b2bc", [128, D], F32)
            xnTm = ssbm(uid + "xnTm", [128, 2 * 8 * 512], BF16)
            xn_b = ssbm(uid + "xnb", [128, 4 * 1024], BF16)
            h1Tm = ssbm(uid + "h1Tm", [128, 16 * 512], BF16)
            tfp = ssbm(uid + "tfp", [128, 1024], F32)
            th = ssbm(uid + "th", [128, 1024], BF16)
            u = ssbm(uid + "u", [128, 1024], F32)
            u0 = ssbm(uid + "u0", [128, 1], F32)
            a0 = ssbm(uid + "a0", [128, 1], F32)
            gt = ssbm(uid + "gt", [128, 2 * 1024], BF16)
            xl = ssbm(uid + "xlm", [128, 3 * 1024], F32)
            ps_h = [psbm(uid + "ph0", [128, 512]), psbm(uid + "ph1", [128, 512])]
            ps_o = [psbm(uid + "po0", [128, 1024]), psbm(uid + "po1", [128, 1024])]
            ctx.ps_tA = psbm(uid + "ptAm", [128, 1024], BF16)
            ctx.ps_tB = psbm(uid + "ptBm", [128, 1024], BF16)
            sgst = [SS["gst0"], SS["gst1"]]

            wT_prepare(ctx, w["mW1"], 16, 1024, w1T, MH)
            wT_prepare(ctx, w["mW2"], 8, 2048, w2T, D)
            y.dma_start(b1c, w["mb1c"]).then_inc(SS["b2"], 16)
            y.dma_start(b2bc[0:1, :], w["mb2"].rearrange("(o d) -> o d", o=1)).then_inc(SS["b2"], 16)
            t.wait_ge(SS["b2"], SS["b2"].n)
            t.wait_ge(SS["trev"], SS["trev"].n)
            t.matmul(ps_o[0][:, 0:512], ctx.ones_f, b2bc[0:1, 0:512],
                     start=True, stop=True)
            t.matmul(ps_o[0][:, 512:1024], ctx.ones_f, b2bc[0:1, 512:1024],
                     start=True, stop=True).then_inc(SS["b2"], 1)
            v.wait_ge(SS["b2"], SS["b2"].n)
            v.tensor_copy(b2bc, ps_o[0]).then_inc(SS["b2"], 1)
            t.wait_ge(SS["b2"], SS["b2"].n)

            b_xn = SS["xn"].n
            b_th = SS["act1"].n
            sgst_mark = [0, 0]
            xn_tr_mark = []
            for st in range(NST):
                par = st % 2
                for tt in range(4):
                    ti = st * 4 + tt
                    sl = ti % 3
                    if ti >= 3:
                        y.wait_ge(SS["xn"], b_xn + ti - 2)
                    y.dma_start(xl[:, sl * 1024:(sl + 1) * 1024], xs[ti]
                                ).then_inc(sxl[sl], 16)
                    v.wait_ge(sxl[sl], sxl[sl].n)
                    if ti >= 4:
                        v.wait_ge(SS["tr"], xn_tr_mark[ti - 4])
                    xb = xn_b[:, (ti % 4) * 1024:(ti % 4 + 1) * 1024]
                    ln_tile(ctx, xl[:, sl * 1024:(sl + 1) * 1024], xb).then_inc(SS["xn"], 1)
                    t.wait_ge(SS["xn"], SS["xn"].n)
                    srcs = [xb[:, j * 128:(j + 1) * 128] for j in range(8)]
                    dsts = []
                    for r in range(2):
                        dsts.append(xnTm.rearrange("p (b2 ic q) -> p (b2 ic) q",
                                                   b2=2, q=512)
                                    [:, par * 8 + 4 * r: par * 8 + 4 * r + 4,
                                     tt * 128:(tt + 1) * 128])
                    transpose_rounds(ctx, srcs, dsts)
                    xn_tr_mark.append(SS["tr"].n)
                    v.wait_ge(SS["trev"], SS["trev"].n)
                    r0 = xnTm[0:1, par * 4096 + tt * 128: par * 4096 + (tt + 1) * 128]
                    v.tensor_scalar(r0, r0, -1.0, None, ALU.add).then_inc(SS["pp"], 1)
                t.wait_ge(SS["pp"], SS["pp"].n)
                for hc in range(16):
                    bank = hc % 2
                    if SS["act1"].n - b_th >= 2:
                        t.wait_ge(SS["act1"], SS["act1"].n - 1)
                    for ic in range(8):
                        inst = t.matmul(ps_h[bank],
                                        w1T[:, ic * MH + hc * 128: ic * MH + (hc + 1) * 128],
                                        xnTm[:, par * 4096 + ic * 512:
                                             par * 4096 + (ic + 1) * 512],
                                        start=(ic == 0), stop=(ic == 7))
                    inst.then_inc(SS["h1"], 1)
                    s.wait_ge(SS["h1"], SS["h1"].n)
                    s.activation(h1Tm[:, hc * 512:(hc + 1) * 512], ps_h[bank], AF.Tanh,
                                 bias=b1c[:, hc:hc + 1]).then_inc(SS["act1"], 1)
                t.wait_ge(SS["act1"], SS["act1"].n)
                for tt in range(4):
                    ti = st * 4 + tt
                    bank = tt % 2
                    if ti >= 2:
                        t.wait_ge(SS["mm2"], SS["mm2"].n - 1)
                    for hc in range(16):
                        t.matmul(ps_o[bank][:, 0:512],
                                 h1Tm[:, hc * 512 + tt * 128: hc * 512 + (tt + 1) * 128],
                                 w2T[:, hc * D: hc * D + 512],
                                 start=(hc == 0), stop=(hc == 15))
                        inst = t.matmul(ps_o[bank][:, 512:1024],
                                        h1Tm[:, hc * 512 + tt * 128:
                                             hc * 512 + (tt + 1) * 128],
                                        w2T[:, hc * D + 512: hc * D + 1024],
                                        start=(hc == 0), stop=(hc == 15))
                    inst.then_inc(SS["mm2"], 1)
                    v.wait_ge(SS["mm2"], SS["mm2"].n)
                    v.tensor_tensor(tfp, ps_o[bank], b2bc, ALU.add).then_inc(SS["mm2"], 1)
                    s.wait_ge(SS["mm2"], SS["mm2"].n)
                    s.activation(th, tfp, AF.Tanh).then_inc(SS["act1"], 1)
                    gsl = ti % 2
                    v.wait_ge(SS["act1"], SS["act1"].n)
                    if ti >= 2:
                        v.wait_ge(sgst[gsl], sgst_mark[gsl])
                    v.tensor_scalar(u, th, 0.1, 1.0, ALU.mult, ALU.add)
                    v.tensor_scalar(u0, th[:, 0:1], 0.1, -1.0, ALU.mult, ALU.add
                                    ).then_inc(SS["g1"], 1)
                    s.wait_ge(SS["g1"], SS["g1"].n)
                    if ti >= 2:
                        s.wait_ge(sgst[gsl], sgst_mark[gsl])
                    s.activation(gt[:, gsl * 1024:(gsl + 1) * 1024], u, AF.Abs)
                    s.activation(a0, u0, AF.Abs).then_inc(SS["g2"], 1)
                    v.wait_ge(SS["g2"], SS["g2"].n)
                    v.tensor_scalar(gt[:, gsl * 1024: gsl * 1024 + 1], a0, -1.0, None,
                                    ALU.mult).then_inc(SS["g3"], 1)
                    g.wait_ge(SS["g3"], SS["g3"].n)
                    g.dma_start(gsp[ti], gt[:, gsl * 1024:(gsl + 1) * 1024]
                                ).then_inc(sgst[gsl], 16)
                    sgst_mark[gsl] = sgst[gsl].n
            for i_ in range(2):
                y.wait_ge(sgst[i_], sgst[i_].n)
        do_barrier(ctx)

    with ExitStack() as paf:
        ssaf = lambda n, sh, dt: paf.enter_context(nc.sbuf_tensor(n, sh, dt))[:]
        vaug = ssaf(uid + "vaug", [128, NT * 1040], BF16)
        woT = ssaf(uid + "woT", [128, 8 * 1024], BF16)

        with ExitStack() as sak:
            psba = lambda n, sh, dt=F32: sak.enter_context(nc.psum_tensor(n, sh, dt))[:]
            ssba = lambda n, sh, dt: sak.enter_context(nc.sbuf_tensor(n, sh, dt))[:]
            if kind == "a":
                wq_d, wk_d, wv_d, wo_d = w["wq"], w["wk"], w["wv"], w["wo"]
            else:
                wq_d, wk_d, wv_d, wo_d = w["bWq"], w["bWk"], w["bWv"], w["bWo"]
            wqT = ssba(uid + "wqT", [128, 8 * 1024], BF16)
            wkT = ssba(uid + "wkT", [128, 8 * 1024], BF16)
            wvT = ssba(uid + "wvT", [128, 8 * 1024], BF16)
            xnT_t = ssba(uid + "xnTt", [128, 4 * 1024], BF16)
            qksb = ssba(uid + "qksb", [128, 1024], BF16)
            qrot = ssba(uid + "qrot", [128, 1024], BF16)
            krot = ssba(uid + "krot", [128, 1024], BF16)
            qstage = ssba(uid + "qstg", [128, 1024], BF16)
            kstage = ssba(uid + "kstg", [128, 1024], BF16)
            xl = ssba(uid + "xla", [128, 2 * 1024], F32)
            if kind == "a":
                xn_b = ssba(uid + "xnb2", [128, 4 * 1024], BF16)
                rt1 = ssba(uid + "rt1", [128, 512], BF16)
                rt2 = ssba(uid + "rt2", [128, 512], BF16)
                ctok = ssba(uid + "ctok", [128, NT * 32], F32)
                stok = ssba(uid + "stok", [128, NT * 32], F32)
                y.dma_start(ctok.rearrange("p (n r) -> p n r", r=32),
                            ctx.ctok_d.rearrange("(n p) r -> p n r", p=128)
                            ).then_inc(SS["b2"], 16)
                y.dma_start(stok.rearrange("p (n r) -> p n r", r=32),
                            ctx.stok_d.rearrange("(n p) r -> p n r", p=128)
                            ).then_inc(SS["b2"], 16)
                v.wait_ge(SS["b2"], SS["b2"].n)
            else:
                xn_f = ssba(uid + "xnf", [128, 1024], F32)
                spbf = ssba(uid + "spbf", [128, 4 * 1024], BF16)
                ssqc = ssba(uid + "ssqc", [128, 4], F32)
                y0c = ssba(uid + "y0c", [128, 4], F32)
                mc = ssba(uid + "mc", [128, 4], F32)
                t4c = ssba(uid + "t4c", [128, 4], F32)
                dnc = ssba(uid + "dnc", [128, 4], F32)
                dcc = ssba(uid + "dcc", [128, 4], F32)
                acol = ssba(uid + "acol", [128, NT], F32)
                d0col = ssba(uid + "d0col", [128, NT], F32)
                d0bf = ssba(uid + "d0bf", [128, 1], BF16)
                gtl = ssba(uid + "gtl", [128, 4 * 1024], BF16)
            ps_q = psba(uid + "psq", [128, 1024])
            ps_k = psba(uid + "psk", [128, 1024])
            ps_v = psba(uid + "psv", [128, 1024])
            ctx.ps_tA = psba(uid + "ptAa", [128, 1024], BF16)
            ctx.ps_tB = psba(uid + "ptBa", [128, 1024], BF16)
            ssp = [SS["sp0"], SS["sp1"]]
            sgl = [SS["gl0"], SS["gl1"], SS["gl2"], SS["gl3"]]

            wT_prepare(ctx, wq_d, 8, 1024, wqT, 1024, perm=(kind == "a"))
            wT_prepare(ctx, wk_d, 8, 1024, wkT, 1024, perm=(kind == "a"))
            wT_prepare(ctx, wv_d, 8, 1024, wvT, 1024)
            wT_prepare(ctx, wo_d, 8, 1024, woT, 1024)

            v.memset(vaug.rearrange("p (c h e) -> p (c h) e", c=NT, h=16)[:, :, 64:65], 1.0)
            v.nop().then_inc(SS["evA"], 1)

            b_xn = SS["xn"].n
            b_evA = SS["evA"].n
            tr_mark = [0] * NT
            ssp_mark = [0, 0]
            sgl_mark = [0] * NT
            for st in range(NST):
                for tt in range(4):
                    ti = st * 4 + tt
                    sl = ti % 2
                    if ti >= 2:
                        y.wait_ge(SS["xn"], b_xn + ti - 1)
                    y.dma_start(xl[:, sl * 1024:(sl + 1) * 1024], xs[ti]).then_inc(sxl[sl], 16)
                    if kind == "b":
                        gl = ti % 4
                        if ti >= 4:
                            y.wait_ge(SS["evA"], b_evA + 3 * ti - 11)
                        y.dma_start(gtl[:, gl * 1024:(gl + 1) * 1024], gsp[ti]
                                    ).then_inc(sgl[gl], 16)
                        sgl_mark[ti] = sgl[gl].n
                    v.wait_ge(sxl[sl], sxl[sl].n)
                    if ti >= 4:
                        v.wait_ge(SS["tr"], tr_mark[ti - 4])
                    if kind == "a":
                        xb = xn_b[:, (ti % 4) * 1024:(ti % 4 + 1) * 1024]
                        ln_tile(ctx, xl[:, sl * 1024:(sl + 1) * 1024], xb).then_inc(SS["xn"], 1)
                        src_t = xb
                    else:
                        ln_tile(ctx, xl[:, sl * 1024:(sl + 1) * 1024], xn_f)
                        spb = spbf[:, (ti % 4) * 1024:(ti % 4 + 1) * 1024]
                        v.tensor_scalar(spb, xn_f, 8.0, -8.0, ALU.min, ALU.max
                                        ).then_inc(SS["xn"], 1)
                        ssq_via_bn(ctx, spb, ssqc[:, tt:tt + 1])
                        src_t = spb
                    t.wait_ge(SS["xn"], SS["xn"].n)
                    srcs = [src_t[:, j * 128:(j + 1) * 128] for j in range(8)]
                    dsts = []
                    for r in range(2):
                        dsts.append(xnT_t.rearrange("p (b4 ic) -> p b4 ic", ic=128)
                                    [:, (ti % 4) * 8 + 4 * r: (ti % 4) * 8 + 4 * r + 4, :])
                    transpose_rounds(ctx, srcs, dsts)
                    tr_mark[ti] = SS["tr"].n
                if kind == "b":
                    v.nop().then_inc(SS["ch1"], 1)
                    s.wait_ge(SS["ch1"], SS["ch1"].n)
                    s.activation(y0c, ssqc, AF.Sqrt,
                                 bias=ctx.consts[:, C_1PEPS:C_1PEPS + 1]).then_inc(SS["ch2"], 1)
                    v.wait_ge(SS["ch2"], SS["ch2"].n)
                    ve(ctx, v.tensor_scalar(mc, y0c, 1.0 + 1e-6, None, ALU.max))
                    v.tensor_mul(t4c, mc, mc).then_inc(SS["ch1"], 1)
                    s.wait_ge(SS["ch1"], SS["ch1"].n)
                    s.activation(t4c, t4c, AF.Sqrt,
                                 bias=ctx.consts[:, C_NEG1:C_NEG1 + 1]).then_inc(SS["ch2"], 1)
                    v.wait_ge(SS["ch2"], SS["ch2"].n)
                    ve(ctx, v.tensor_sub(d0col[:, st * 4:st * 4 + 4], y0c, mc))
                    ve(ctx, v.tensor_mul(dnc, d0col[:, st * 4:st * 4 + 4],
                                         d0col[:, st * 4:st * 4 + 4]))
                    ve(ctx, v.tensor_sub(dnc, ssqc, dnc))
                    ve(ctx, v.tensor_scalar(dnc, dnc, 1e-6, None, ALU.max))
                    v.tensor_add(dcc, mc, t4c).then_inc(SS["ch1"], 1)
                    s.wait_ge(SS["ch1"], SS["ch1"].n)
                    s.activation(dnc, dnc, AF.Sqrt)
                    s.activation(dcc, dcc, AF.Ln).then_inc(SS["ch2"], 1)
                    v.wait_ge(SS["ch2"], SS["ch2"].n)
                    ve(ctx, v.tensor_scalar(dcc, dcc, 0.001, None, ALU.max))
                    ve(ctx, v.tensor_scalar(dnc, dnc, 1e-6, None, ALU.add))
                    ve(ctx, v.reciprocal(dnc, dnc))
                    v.tensor_mul(acol[:, st * 4:st * 4 + 4], dcc, dnc).then_inc(SS["ch1"], 1)
                for tt in range(4):
                    ti = st * 4 + tt
                    if kind == "b":
                        v.wait_ge(SS["ch1"], SS["ch1"].n)
                        v.wait_ge(SS["tr"], SS["tr"].n)
                        v.tensor_copy(d0bf, d0col[:, ti:ti + 1]).then_inc(SS["ch1"], 1)
                        t.wait_ge(SS["ch1"], SS["ch1"].n)
                        transpose_col(ctx, d0bf,
                                      xnT_t[0:1, (ti % 4) * 1024:(ti % 4) * 1024 + 128])
                    t.wait_ge(SS["trev"], SS["trev"].n)
                    for mi, (psm, wTm) in enumerate(((ps_q, wqT), (ps_k, wkT), (ps_v, wvT))):
                        if ti >= 1:
                            t.wait_ge(SS["evA"], b_evA + 3 * ti + mi - 2)
                        for ic in range(8):
                            lt = xnT_t[:, (ti % 4) * 1024 + ic * 128:
                                       (ti % 4) * 1024 + (ic + 1) * 128]
                            t.matmul(psm[:, 0:512], lt, wTm[:, ic * 1024: ic * 1024 + 512],
                                     start=(ic == 0), stop=(ic == 7))
                            inst = t.matmul(psm[:, 512:1024], lt,
                                            wTm[:, ic * 1024 + 512:(ic + 1) * 1024],
                                            start=(ic == 0), stop=(ic == 7))
                        inst.then_inc(SS["mm"], 1)
                    v.wait_ge(SS["tr"], SS["tr"].n)
                    v.wait_ge(SS["mm"], SS["mm"].n - 2)
                    if kind == "a":
                        cb = ctok[:, ti * 32:(ti + 1) * 32].unsqueeze(1).to_broadcast(
                            (128, 16, 32))
                        sbq = stok[:, ti * 32:(ti + 1) * 32].unsqueeze(1).to_broadcast(
                            (128, 16, 32))
                        for psrc, dstb in ((ps_q, qrot), (ps_k, krot)):
                            if psrc is ps_k:
                                v.wait_ge(SS["mm"], SS["mm"].n - 1)
                            ve(ctx, v.tensor_copy(qksb, psrc))
                            ev = qksb.rearrange("p (h q) -> p h q", q=64)[:, :, 0:32]
                            od = qksb.rearrange("p (h q) -> p h q", q=64)[:, :, 32:64]
                            r1v = rt1.rearrange("p (h q) -> p h q", q=32)
                            r2v = rt2.rearrange("p (h q) -> p h q", q=32)
                            do = dstb.rearrange("p (h q) -> p h q", q=64)
                            v.tensor_tensor(r1v, ev, cb, ALU.mult)
                            v.tensor_tensor(r2v, od, sbq, ALU.mult)
                            v.tensor_tensor(do[:, :, 0:32], r1v, r2v, ALU.subtract)
                            v.tensor_tensor(r1v, ev, sbq, ALU.mult)
                            v.tensor_tensor(r2v, od, cb, ALU.mult)
                            v.tensor_tensor(do[:, :, 32:64], r1v, r2v, ALU.add
                                            ).then_inc(SS["evA"], 1)
                        v.wait_ge(SS["mm"], SS["mm"].n)
                        v.tensor_copy(vaug[:, ti * 1040:(ti + 1) * 1040].rearrange(
                            "p (h e) -> p h e", h=16)[:, :, 0:64],
                            ps_v.rearrange("p (h e) -> p h e", h=16)).then_inc(SS["evA"], 1)
                    else:
                        gl = ti % 4
                        v.wait_ge(sgl[gl], sgl_mark[ti])
                        v.scalar_tensor_tensor(
                            qrot, ps_q, acol[:, ti:ti + 1],
                            gtl[:, gl * 1024:(gl + 1) * 1024], ALU.mult, ALU.mult
                            ).then_inc(SS["evA"], 1)
                        v.wait_ge(SS["mm"], SS["mm"].n - 1)
                        v.tensor_scalar(krot, ps_k, acol[:, ti:ti + 1], None, ALU.mult
                                        ).then_inc(SS["evA"], 1)
                        v.wait_ge(SS["mm"], SS["mm"].n)
                        v.tensor_scalar(vaug[:, ti * 1040:(ti + 1) * 1040].rearrange(
                            "p (h e) -> p h e", h=16)[:, :, 0:64],
                            ps_v.rearrange("p (h e) -> p h e", h=16),
                            acol[:, ti:ti + 1], None, ALU.mult).then_inc(SS["evA"], 1)
                    t.wait_ge(SS["evA"], SS["evA"].n)
                    par = ti % 2
                    v.wait_ge(ssp[par], ssp_mark[par])
                    for rot, stage in ((qrot, qstage), (krot, kstage)):
                        srcs = [rot[:, j * 128:(j + 1) * 128] for j in range(8)]
                        dsts = []
                        for r in range(2):
                            dsts.append(stage.rearrange("p (b2 ic) -> p b2 ic", ic=128)
                                        [:, 4 * r: 4 * r + 4, :])
                        transpose_rounds(ctx, srcs, dsts)
                    g.wait_ge(SS["trev"], SS["trev"].n)
                    for oc in range(8):
                        g.dma_start(qsp[oc][:, ti * 128:(ti + 1) * 128],
                                    qstage[:, oc * 128:(oc + 1) * 128]).then_inc(ssp[par], 16)
                        g.dma_start(ksp[oc][:, ti * 128:(ti + 1) * 128],
                                    kstage[:, oc * 128:(oc + 1) * 128]).then_inc(ssp[par], 16)
                    ssp_mark[par] = ssp[par].n
            for i_ in range(2):
                y.wait_ge(ssp[i_], ssp[i_].n)
            if DBG and kind == "b":
                for nm, ap in (("d_acol", acol), ("d_d0col", d0col), ("d_ssqc", ssqc),
                               ("d_wqT", wqT), ("d_wkT", wkT), ("d_wvT", wvT),
                               ("d_xnTt", xnT_t),
                               ("d_y0c", y0c), ("d_mc", mc), ("d_t4c", t4c),
                               ("d_dnc", dnc), ("d_dcc", dcc)):
                    dd = nc.dram_tensor(nm + uid, list(ap.shape), ap.dtype,
                                        kind="ExternalOutput").ap()
                    y.dma_start(dd, ap).then_inc(SS["b2"], 16)
                y.wait_ge(SS["b2"], SS["b2"].n)
        do_barrier(ctx)

        with ExitStack() as sfk:
            psbf = lambda n, sh, dt=F32: sfk.enter_context(nc.psum_tensor(n, sh, dt))[:]
            ssbf = lambda n, sh, dt: sfk.enter_context(nc.sbuf_tensor(n, sh, dt))[:]
            kT = ssbf(uid + "kT", [128, NP * S], BF16)
            pT = ssbf(uid + "pT", [128, 2 * 2048], BF16)
            qtl = ssbf(uid + "qtl", [128, 3 * 512], BF16)
            aout = ssbf(uid + "aout", [128, 8 * 512], BF16)
            lnrow = ssbf(uid + "lnrow", [1, 1024], BF16)
            recrow = ssbf(uid + "recrow", [1, 1024], BF16)
            bcs = ssbf(uid + "bcs", [64, 1024], BF16)
            xr = ssbf(uid + "xr", [128, 2 * 1024], F32)
            xo = ssbf(uid + "xo", [128, 2 * 1024], F32)
            if kind == "b":
                zf = ssbf(uid + "zf", [128, 1024], F32)
                zw = ssbf(uid + "zw", [128, 1], F32)
                s0 = ssbf(uid + "s0", [128, 1], F32)
            ps_qk = [psbf(uid + "pqk0", [128, 1024]), psbf(uid + "pqk1", [128, 1024])]
            ps_pv = psbf(uid + "ppv", [128, 1024])
            ps_wo = psbf(uid + "pwo", [128, 1024])
            sqt = [SS["qt0"], SS["qt1"], SS["qt2"]]
            sxr = [SS["xr0"], SS["xr1"], SS["xr2"]]

            for p in range(NP):
                y.dma_start(kT[:, p * S:(p + 1) * S], ksp[p]).then_inc(SS["kld"], 16)
            t.wait_ge(SS["kld"], SS["kld"].n)

            b_qk = SS["qk"].n
            b_exp = SS["exp"].n
            b_xnew = SS["xnew"].n
            sxo_mark = [0, 0, 0]
            gg = 0
            it = 0

            def finish_pair(pp):
                t.wait_ge(SS["xnew"], SS["xnew"].n)
                t.wait_ge(SS["bc"], SS["bc"].n)
                t.wait_ge(SS["rc"], SS["rc"].n)
                t.matmul(ps_wo[0:64, 0:512], ctx.ones_bf[0:1, 0:64],
                         recrow[0:1, 0:512], start=True, stop=True)
                t.matmul(ps_wo[0:64, 512:1024], ctx.ones_bf[0:1, 0:64],
                         recrow[0:1, 512:1024], start=True, stop=True).then_inc(SS["rep"], 1)
                v.wait_ge(SS["rep"], SS["rep"].n)
                ve(ctx, v.tensor_copy(bcs, ps_wo[0:64, 0:1024]))
                v.nop().then_inc(SS["bc"], 1)
                if pp == 0:
                    v.wait_ge(SS["wo"], SS["wo"].n)
                v.tensor_tensor(aout[0:64, pp * 512:(pp + 1) * 512],
                                ps_pv[0:64, 0:512], bcs[0:64, 0:512], ALU.mult)
                v.tensor_tensor(aout[64:128, pp * 512:(pp + 1) * 512],
                                ps_pv[0:64, 512:1024], bcs[0:64, 512:1024], ALU.mult
                                ).then_inc(SS["ao"], 1)

            def emit_wo_tt(wqt, tt):
                ti = wqt * 4 + tt
                if tt == 0:
                    t.wait_ge(SS["ao"], SS["ao"].n)
                    t.wait_ge(SS["bc"], SS["bc"].n)
                else:
                    t.wait_ge(SS["xnew"], SS["xnew"].n)
                for cc in range(8):
                    t.matmul(ps_wo[:, 0:512],
                             aout[:, cc * 512 + tt * 128: cc * 512 + (tt + 1) * 128],
                             woT[:, cc * 1024: cc * 1024 + 512],
                             start=(cc == 0), stop=(cc == 7))
                    inst = t.matmul(ps_wo[:, 512:1024],
                                    aout[:, cc * 512 + tt * 128: cc * 512 + (tt + 1) * 128],
                                    woT[:, cc * 1024 + 512: (cc + 1) * 1024],
                                    start=(cc == 0), stop=(cc == 7))
                inst.then_inc(SS["wo"], 1)
                slr = ti % 2
                if ti >= 2:
                    y.wait_ge(SS["xnew"], b_xnew + ti - 1)
                y.dma_start(xr[:, slr * 1024:(slr + 1) * 1024], xs[ti]).then_inc(sxr[slr], 16)
                v.wait_ge(SS["wo"], SS["wo"].n)
                v.wait_ge(sxr[slr], sxr[slr].n)
                slo = ti % 2
                if ti >= 2:
                    v.wait_ge(sxo[slo], sxo_mark[slo])
                xov = xo[:, slo * 1024:(slo + 1) * 1024]
                if kind == "a":
                    v.tensor_tensor(xov, ps_wo, xr[:, slr * 1024:(slr + 1) * 1024],
                                    ALU.add).then_inc(SS["xnew"], 1)
                else:
                    ve(ctx, v.tensor_scalar(zf, ps_wo, 0.1, None, ALU.mult))
                    ve(ctx, v.tensor_scalar(zf[:, 0:1], zf[:, 0:1],
                                            float(np.sqrt(np.float32(1.0) + np.float32(1e-6))),
                                            None, ALU.add))
                    ve(ctx, v.tensor_scalar(zf, zf, 8.0, -8.0, ALU.min, ALU.max))
                    ve(ctx, ssq_via_bn(ctx, zf, s0[:, 0:1]))
                    v.tensor_scalar(zw, s0, 1.0 + 1e-6, None, ALU.add).then_inc(SS["tl1"], 1)
                    s.wait_ge(SS["tl1"], SS["tl1"].n)
                    se(ctx, s.activation(zw, zw, AF.Ln))
                    s.activation(zw, zw, AF.Exp, scale=0.5).then_inc(SS["tl2"], 1)
                    v.wait_ge(SS["tl2"], SS["tl2"].n)
                    ve(ctx, v.tensor_copy(zf[:, 0:1], zw))
                    v.tensor_tensor(xov, zf, xr[:, slr * 1024:(slr + 1) * 1024],
                                    ALU.add).then_inc(SS["xnew"], 1)
                g.wait_ge(SS["xnew"], SS["xnew"].n)
                g.dma_start(ds[ti], xov).then_inc(sxo[slo], 16)
                sxo_mark[slo] = sxo[slo].n

            pend = None
            pend_wo = None
            for qt in range(NST):
                for p in range(NP):
                    it += 1
                    qslot = (it - 1) % 3
                    if it >= 4:
                        y.wait_ge(SS["qk"], b_qk + 32 * (it - 3))
                    y.dma_start(qtl[:, qslot * 512:(qslot + 1) * 512],
                                qsp[p][:, qt * 512:(qt + 1) * 512]).then_inc(sqt[qslot], 16)
                    t.wait_ge(sqt[qslot], sqt[qslot].n)
                    gg0 = gg
                    for kc in range(NT):
                        gg += 1
                        if gg > 2:
                            t.wait_ge(SS["exp"], b_exp + gg - 2)
                        k0 = p * S + kc * 128
                        cA = (gg - 1) % 2
                        t.matmul(ps_qk[cA][:, 0:512], kT[0:64, k0:k0 + 128],
                                 qtl[0:64, qslot * 512:(qslot + 1) * 512],
                                 start=True, stop=True, tile_position=(0, 0))
                        t.matmul(ps_qk[cA][:, 512:1024], kT[64:128, k0:k0 + 128],
                                 qtl[64:128, qslot * 512:(qslot + 1) * 512],
                                 start=True, stop=True, tile_position=(64, 0)
                                 ).then_inc(SS["qk"], 1)
                        s.wait_ge(SS["qk"], SS["qk"].n)
                        pb = ((gg - 1) % 4) * 1024
                        if kind == "a":
                            s.activation(pT[:, pb:pb + 1024], ps_qk[cA], AF.Exp, scale=0.125,
                                         bias=ctx.consts[:, C_EXPSHIFT:C_EXPSHIFT + 1]
                                         ).then_inc(SS["exp"], 1)
                        else:
                            s.activation(pT[:, pb:pb + 1024], ps_qk[cA], AF.Exp,
                                         scale=0.125).then_inc(SS["exp"], 1)
                        if kc == 1 and pend is not None:
                            finish_pair(pend)
                            pend = None
                        if pend_wo is not None and kc in (2, 4, 6, 8):
                            emit_wo_tt(pend_wo, (kc - 2) // 2)
                            if kc == 8:
                                pend_wo = None
                        if kc >= 3:
                            if kc == 3:
                                t.wait_ge(SS["ao"], SS["ao"].n)
                            pb2 = ((gg - 4) % 4) * 1024
                            for h2 in range(2):
                                hh = 2 * p + h2
                                t.matmul(ps_pv[0:65, h2 * 512:(h2 + 1) * 512],
                                         vaug[:, (kc - 3) * 1040 + hh * 65:
                                              (kc - 3) * 1040 + (hh + 1) * 65],
                                         pT[:, pb2 + h2 * 512: pb2 + (h2 + 1) * 512],
                                         start=(kc == 3), stop=False)
                    for kc in (NT - 3, NT - 2, NT - 1):
                        t.wait_ge(SS["exp"], b_exp + gg0 + kc + 1)
                        pb2 = ((gg0 + kc) % 4) * 1024
                        for h2 in range(2):
                            hh = 2 * p + h2
                            inst = t.matmul(ps_pv[0:65, h2 * 512:(h2 + 1) * 512],
                                            vaug[:, kc * 1040 + hh * 65:
                                                 kc * 1040 + (hh + 1) * 65],
                                            pT[:, pb2 + h2 * 512: pb2 + (h2 + 1) * 512],
                                            start=False, stop=(kc == NT - 1))
                    inst.then_inc(SS["pv"], 1)
                    s.wait_ge(SS["pv"], SS["pv"].n)
                    s.wait_ge(SS["rep"], SS["rep"].n)
                    se(ctx, s.activation(lnrow, ps_pv[64:65, 0:1024], AF.Ln))
                    s.activation(recrow, lnrow, AF.Exp, scale=-1.0).then_inc(SS["rc"], 1)
                    pend = p
                pend_wo = qt
            finish_pair(pend)
            pend = None
            for tt in range(4):
                emit_wo_tt(NST - 1, tt)
            for sl in range(2):
                y.wait_ge(sxo[sl], sxo[sl].n)
            if DBG:
                for nm, ap in (("d_vaug", vaug), ("d_kT", kT)):
                    dd = nc.dram_tensor(nm + uid, list(ap.shape), ap.dtype,
                                        kind="ExternalOutput").ap()
                    y.dma_start(dd, ap).then_inc(SS["b2"], 16)
                y.wait_ge(SS["b2"], SS["b2"].n)


def host_consts():
    c = np.zeros((128, 8), dtype=np.float32)
    c[:, C_EXPSHIFT] = -16.0
    c[:, C_LNEPS] = 1e-5
    c[:, C_1PEPS] = np.float32(1.0) + np.float32(1e-6)
    c[:, C_NEG1] = -1.0
    return c


def host_rope():
    inv = 1.0 / 10000.0 ** (np.arange(0, HD, 2, dtype=np.float32) / HD)
    tpos = np.arange(S, dtype=np.float32)
    ang = np.outer(tpos, inv)
    return np.cos(ang).astype(np.float32), np.sin(ang).astype(np.float32)


def make_inmaps(inputs):
    cos, sin = host_rope()
    base = dict(
        wq=inputs["L0_attn_a_wq"], wk=inputs["L0_attn_a_wk"],
        wv=inputs["L0_attn_a_wv"], wo=inputs["L0_attn_a_wo"],
        fw1=inputs["L0_ffn_w1"], fb1=inputs["L0_ffn_b1"],
        fw2=inputs["L0_ffn_w2"], fb2=inputs["L0_ffn_b2"],
        mW1=inputs["L0_attn_b_W1"], mb1=inputs["L0_attn_b_b1"],
        fb1c=np.ascontiguousarray(np.asarray(inputs["L0_ffn_b1"], dtype=np.float32).reshape(32, 128).T),
        mb1c=np.ascontiguousarray(np.asarray(inputs["L0_attn_b_b1"], dtype=np.float32).reshape(16, 128).T),
        mW2=inputs["L0_attn_b_W2"], mb2=inputs["L0_attn_b_b2"],
        bWq=inputs["L0_attn_b_Wq"], bWk=inputs["L0_attn_b_Wk"],
        bWv=inputs["L0_attn_b_Wv"], bWo=inputs["L0_attn_b_Wo"],
        ident=np.eye(128, dtype=ml_dtypes.bfloat16),
        identf=np.eye(128, dtype=np.float32),
        ones_bf=np.ones((1, 128), dtype=ml_dtypes.bfloat16),
        ones_f=np.ones((1, 128), dtype=np.float32),
        consts=host_consts(), ctok=cos, stok=sin,
    )
    base = {k: np.ascontiguousarray(np.asarray(v)) for k, v in base.items()}
    x = np.ascontiguousarray(np.asarray(inputs["L0_attn_a_x"], dtype=np.float32))
    maps = []
    for b in range(8):
        m = dict(base)
        m["x"] = np.ascontiguousarray(x[b])
        maps.append(m)
    return maps


def kernel(**inputs):
    nc = build(4)
    maps = make_inmaps(inputs)
    res = run_bass_kernel_spmd(nc, maps, core_ids=list(range(8)))
    return np.stack([res.results[b]["out"] for b in range(8)], axis=0)
